# version 41
# baseline (speedup 1.0000x reference)
import numpy as np
import ml_dtypes
from contextlib import ExitStack

import concourse.bass as bass
import concourse.mybir as mybir
from concourse.bass_utils import run_bass_kernel_spmd

F32 = mybir.dt.float32
BF16 = mybir.dt.bfloat16
AF = mybir.ActivationFunctionType
ALU = mybir.AluOpType
AX = mybir.AxisListType
BF = ml_dtypes.bfloat16

D = 1024
T = 2048
NCORES = 8
SEQ_PER_CORE = 2
BLK = 256
NBLK_SEQ = T // BLK
NBLK = SEQ_PER_CORE * NBLK_SEQ
NHEAD = 16
DH = 64
WINDOW = 512
NP = 16
C_EXP = 16.0
SCALE = 0.125
LN_EPS = 1e-5
E = 8
HF = 256
RING = 6

ENGS = ("sync", "gpsimd", "vector", "scalar", "tensor")


def _masks_for_blocks():
    tiles, keys, mapping = [], {}, {}
    for wb in range(NBLK_SEQ):
        gmin = max(0, 2 * wb - 4)
        gcs = list(range(gmin, 2 * wb + 2))
        for ui, i in enumerate(range(0, len(gcs), 2)):
            pair = gcs[i:i + 2]
            cols = []
            for gc in pair:
                k = 128 * gc + np.arange(128)[:, None]
                q = 256 * wb + np.arange(256)[None, :]
                band = (np.abs(q - k) <= WINDOW) & (q >= NP) & (k >= NP) & (q >= k)
                cols.append(((k < NP) | band))
            allowed = np.concatenate(cols, axis=1)
            if allowed.all():
                mapping[(wb, ui)] = None
                continue
            key = allowed.tobytes()
            if key not in keys:
                keys[key] = len(tiles)
                tiles.append(allowed.astype(np.float32))
            mapping[(wb, ui)] = keys[key]
    return np.stack(tiles), mapping


class Sched:
    def __init__(self):
        self.ops = {e: [] for e in ENGS}
        self.cnt = {}
        self.sems = {}
        self.log = {e: [] for e in ENGS}

    def produce(self, eng, fn, sem, inc=1):
        self.cnt[sem] = self.cnt.get(sem, 0) + inc
        tgt = self.cnt[sem]

        def wrapped(EE, fn=fn, sem=sem, inc=inc):
            fn(EE).then_inc(self.sems[sem], inc)
        self.ops[eng].append(wrapped)
        self.log[eng].append(("i", sem, inc))
        return tgt

    def run(self, eng, fn):
        self.ops[eng].append(fn)

    def wait(self, eng, sem, val):
        if val is None or val <= 0:
            return
        have = self.cnt.get(sem, 0)
        assert val <= have, f"wait {eng} {sem} {val} > emitted {have}"
        self.ops[eng].append(lambda EE, sem=sem, val=val: EE.wait_ge(self.sems[sem], val))
        self.log[eng].append(("w", sem, val))

    def c(self, sem):
        return self.cnt.get(sem, 0)

    def simulate(self):
        pos = {e: 0 for e in ENGS}
        val = {}
        progress = True
        while progress:
            progress = False
            for e in ENGS:
                while pos[e] < len(self.log[e]):
                    kind, sem, x = self.log[e][pos[e]]
                    if kind == "i":
                        val[sem] = val.get(sem, 0) + x
                    else:
                        if val.get(sem, 0) < x:
                            break
                    pos[e] += 1
                    progress = True
        stuck = {e: (pos[e], len(self.log[e]), self.log[e][pos[e]] if pos[e] < len(self.log[e]) else None)
                 for e in ENGS if pos[e] < len(self.log[e])}
        return stuck or None


def build_program(mask_tiles, mask_map, nblk_run=NBLK, startup_only=False, stage=99, qkvs=9, dbg=False, pers_mode=2):
    NM = mask_tiles.shape[0]
    nc = bass.Bass("TRN2", target_bir_lowering=False)

    dx = nc.dram_tensor("x5", [NBLK, 128, 8, BLK], F32, kind="ExternalInput").ap()
    dy = nc.dram_tensor("y5", [NBLK, 128, 8, BLK], F32, kind="ExternalOutput").ap()
    dwqkv = nc.dram_tensor("wqkvT", [8, 128, 3072], F32, kind="ExternalInput").ap()
    dwout = nc.dram_tensor("woutT", [8, 128, 1024], F32, kind="ExternalInput").ap()
    dw1 = nc.dram_tensor("w1h", [E, 8, 128, HF], F32, kind="ExternalInput").ap()
    dw2 = nc.dram_tensor("w2h", [E, 2, 128, 1024], F32, kind="ExternalInput").ap()
    dwg = nc.dram_tensor("wgT", [8, 128, E], F32, kind="ExternalInput").ap()
    dlns = nc.dram_tensor("lnsh", [128, E, 8], F32, kind="ExternalInput").ap()
    dlnb = nc.dram_tensor("lnbh", [128, E, 8], F32, kind="ExternalInput").ap()
    db1 = nc.dram_tensor("b1h", [128, E, 2], F32, kind="ExternalInput").ap()
    db2 = nc.dram_tensor("b2h", [E, 1024], F32, kind="ExternalInput").ap()
    dmask = nc.dram_tensor("maskh", [NM, 128, 2 * BLK], BF16, kind="ExternalInput").ap()
    dident = nc.dram_tensor("identh", [128, 128], F32, kind="ExternalInput").ap()
    if dbg:
        dbg_attn = nc.dram_tensor("dbg_attn", [NBLK, 128, 2048], BF16, kind="ExternalOutput").ap()
        dbg_x1 = nc.dram_tensor("dbg_x1", [NBLK, 128, 2048], F32, kind="ExternalOutput").ap()
        dbg_xh = nc.dram_tensor("dbg_xh", [NBLK, 128, 2048], BF16, kind="ExternalOutput").ap()
        dbg_q = nc.dram_tensor("dbg_q", [NBLK, 128, 2048], BF16, kind="ExternalOutput").ap()
        dbg_msk = nc.dram_tensor("dbg_msk", [NBLK, 128, 16], F32, kind="ExternalOutput").ap()
        dbg_mt = nc.dram_tensor("dbg_mt", [NBLK, 8, 256], BF16, kind="ExternalOutput").ap()
        dbg_h = nc.dram_tensor("dbg_h", [NBLK, 128, 512], BF16, kind="ExternalOutput").ap()
        dbg_z2 = nc.dram_tensor("dbg_z2", [NBLK, 128, 2048], F32, kind="ExternalOutput").ap()
        dbg_lg = nc.dram_tensor("dbg_lg", [NBLK, 128, 16], F32, kind="ExternalOutput").ap()
        dbg_mx = nc.dram_tensor("dbg_mx", [NBLK, 128, 2], F32, kind="ExternalOutput").ap()
        dbg_b1 = nc.dram_tensor("dbg_b1", [128, 16], F32, kind="ExternalOutput").ap()
    de8 = nc.dram_tensor("e8h", [8, E, 128], F32, kind="ExternalInput").ap()

    es = ExitStack()

    def sb(name, shape, dt=F32):
        return es.enter_context(nc.sbuf_tensor(name, shape, dt))

    def psmk(name):
        return es.enter_context(nc.psum_tensor(name, [128, 512], F32))

    wqkv = sb("wqkv", [128, 8 * 3072], BF16)
    wout = sb("wout", [128, 8 * 1024], BF16)
    w1p = sb("w1p", [128, E * 8 * HF], BF16)
    w2p = sb("w2p", [128, E * 2 * 1024], BF16)
    lnssb = sb("lnssb", [128, E * 8], F32)
    lnbsb = sb("lnbsb", [128, E * 8], F32)
    b1tot = sb("b1tot", [128, E * 2], F32)
    b2s = sb("b2s", [E, 1024], BF16)
    e8f = sb("e8f", [8, E * 128], BF16)
    maskt = sb("maskt", [128, NM * 2 * BLK], BF16)
    ident = sb("identsb", [128, 128], F32)
    cbias = sb("cbias", [128, 1], F32)
    onescol = sb("onescol", [128, 1], F32)
    onescol16 = sb("onescol16", [128, 1], BF16)
    sqb = sb("sqb", [128, 512], BF16)
    onesr16 = sb("onesr16", [128, 128], BF16)
    den16 = sb("den16", [128, 512], BF16)

    xslot = sb("xslot", [128, 2048], F32)
    bfA = sb("bfA", [128, 2048], BF16)
    qsb = sb("qsb", [128, 8 * BLK], BF16)
    kring = sb("kringsb", [128, 8 * RING * 128], BF16)
    kpers = sb("kpers", [128, 8 * NP], BF16)
    vring = sb("vringsb", [128, RING * 1040], BF16)
    vpers = sb("vpers", [48, 1040], BF16)
    ptA = sb("ptA", [128, 6 * BLK], BF16)
    ptB = sb("ptB", [128, 6 * BLK], BF16)
    ptP = sb("ptP", [48, 512], BF16)
    den = sb("den", [128, 512], F32)
    x1slot = sb("x1slot", [128, 2048], F32)
    xhat = sb("xhatsb", [128, 2048], BF16)
    scrF = sb("scrF", [128, 512], F32)
    Hsb = sb("Hsb", [128, 512], BF16)
    Hsb2 = sb("Hsb2", [128, 512], BF16)
    mskT16 = sb("mskT16", [E, BLK], BF16)
    msksb = sb("msksb", [128, 2 * E], F32)
    maxsb = sb("maxsb", [128, 2], F32)
    wgf = sb("wgf", [128, 8 * E], F32)


    P0 = psmk("P0")
    P1 = psmk("P1")
    P2 = psmk("P2")
    P3 = psmk("P3")
    P4 = psmk("P4")
    P5 = psmk("P5")
    P6 = psmk("P6")
    P7 = psmk("P7")
    PY = [P2, P3, P4, P5]

    S = Sched()
    sems = {}
    for name in ("xd", "gd", "t", "v", "s", "g"):
        sems[name] = es.enter_context(nc.semaphore(name))
    S.sems = sems

    chunk_info = {}
    unit_mask_idx = {}
    for wb in range(NBLK_SEQ):
        gmin = max(0, 2 * wb - 4)
        chunk_info[wb] = [(gc, gc % RING, None)
                          for gc in range(gmin, 2 * wb + 2)]
        for ui in range((len(chunk_info[wb]) + 1) // 2):
            unit_mask_idx[(wb, ui)] = mask_map.get((wb, ui))

    pre_cast0 = None
    S.produce("gpsimd", lambda EE: EE.dma_start(b2s[:], db2), "gd", 16)
    t_smalls = S.produce("gpsimd", lambda EE: EE.dma_start(e8f[:], de8), "gd", 16)
    t_part = {}
    for part in range(3):
        for dc in range(8):
            S.produce("gpsimd", lambda EE, dc=dc, part=part: EE.dma_start(
                wqkv[:, dc * 3072 + part * 1024: dc * 3072 + (part + 1) * 1024],
                dwqkv[dc][:, part * 1024:(part + 1) * 1024]), "gd", 16)
        t_part[part] = S.c("gd")
    for dc in range(8):
        S.produce("gpsimd", lambda EE, dc=dc: EE.dma_start(
            wout[:, dc * 1024:(dc + 1) * 1024], dwout[dc]), "gd", 16)
    t_wout = S.c("gd")
    for e in range(E):
        for hc in range(2):
            S.produce("gpsimd", lambda EE, e=e, hc=hc: EE.dma_start(
                w2p[:, (e * 2 + hc) * 1024:(e * 2 + hc + 1) * 1024], dw2[e, hc]), "gd", 16)
    t_w2 = S.c("gd")

    S.produce("sync", lambda EE: EE.dma_start(ident[:], dident), "xd", 16)
    S.produce("sync", lambda EE: EE.dma_start(wgf[:], dwg.rearrange("c p e -> p c e")), "xd", 16)
    S.produce("sync", lambda EE: EE.dma_start(b1tot[:], db1), "xd", 16)
    S.produce("sync", lambda EE: EE.dma_start(lnssb[:], dlns), "xd", 16)
    t_lnw = S.produce("sync", lambda EE: EE.dma_start(lnbsb[:], dlnb), "xd", 16)
    S.produce("sync", lambda EE: EE.dma_start(
        maskt[:], dmask.rearrange("n p t -> p n t")), "xd", 16)
    x_ready = {}
    x_ready[0] = S.produce("sync", lambda EE: EE.dma_start(xslot[:], dx[0]), "xd", 16)
    t_plain = S.c("xd")

    S.run("vector", lambda EE: EE.memset(cbias[:], -C_EXP))
    S.run("vector", lambda EE: EE.memset(onescol[:], 1.0 / D))
    S.run("vector", lambda EE: EE.memset(onescol16[:], 1.0 / D))
    S.run("vector", lambda EE: EE.memset(onesr16[:], 1.0))
    t_memset = S.produce("vector", lambda EE: EE.memset(
        bass.AP(vring, 64, [[RING * 1040, 128], [1040, RING], [65, NHEAD], [1, 1]]), 1.0),
        "v", 1)
    S.wait("vector", "xd", x_ready[0])
    pre_cast0_v = S.produce("vector", lambda EE: EE.tensor_copy(
        bfA[:], xslot[:]), "v", 1)

    S.wait("vector", "xd", t_lnw)
    S.wait("vector", "xd", t_plain)
    cons_v = {-2: 0, -1: 0}
    cons_t = {-2: 0, -1: 0}
    b1add = {-2: 0, -1: 0}
    for idx in range(16):
        e, h = idx // 2, idx % 2
        buf = idx % 2
        S.wait("sync", "v", max(cons_v[idx - 2], b1add[idx - 2]))
        S.wait("sync", "t", cons_t[idx - 2])
        stg = S.produce("sync", lambda EE, e=e, h=h, buf=buf: EE.dma_start(
            x1slot[:, buf * 1024:(buf + 1) * 1024],
            dw1[e].rearrange("c p h -> p c h")[:, 4 * h:4 * h + 4, :]), "xd", 16)
        S.wait("vector", "xd", stg)
        for k in range(4):
            dc = 4 * h + k
            cons_v[idx] = S.produce(
                "vector", lambda EE, e=e, dc=dc, k=k, buf=buf: EE.tensor_scalar_mul(
                    w1p[:, (e * 8 + dc) * HF:(e * 8 + dc + 1) * HF],
                    x1slot[:, buf * 1024 + k * HF: buf * 1024 + (k + 1) * HF],
                    lnssb[:, e * 8 + dc:e * 8 + dc + 1]), "v", 1)
        S.wait("tensor", "xd", stg)
        S.wait("tensor", "v", b1add[idx - 2])
        for hc in range(2):
            col = hc * 64 + (idx % 2) * 32
            for j in range(4):
                cons_t[idx] = S.produce(
                    "tensor",
                    lambda EE, e=e, h=h, hc=hc, j=j, buf=buf, col=col: EE.matmul(
                        P0[:, col:col + 1],
                        x1slot[:, buf * 1024 + j * HF + hc * 128:
                               buf * 1024 + j * HF + hc * 128 + 128],
                        lnbsb[:, e * 8 + 4 * h + j: e * 8 + 4 * h + j + 1],
                        start=(j == 0), stop=(j == 3)), "t", 1)
        S.wait("vector", "t", cons_t[idx])
        for hc in range(2):
            b1add[idx] = S.produce(
                "vector", lambda EE, e=e, col=hc * 64 + (idx % 2) * 32, hc=hc: EE.tensor_add(
                    b1tot[:, e * 2 + hc:e * 2 + hc + 1],
                    b1tot[:, e * 2 + hc:e * 2 + hc + 1],
                    P0[:, col:col + 1]), "v", 1)
    stage_v_final = S.c("v")

    if dbg:
        S.wait("gpsimd", "v", S.c("v"))
        S.produce("gpsimd", lambda EE: EE.dma_start(dbg_b1, b1tot[:]), "gd", 16)

    S.wait("tensor", "xd", t_plain)
    S.wait("tensor", "gd", t_smalls)
    S.wait("tensor", "v", t_memset)
    S.wait("scalar", "v", t_memset)

    qkv_done = {}
    proj_last_t = {}
    gate_done = {}
    ffn_free = {}

    def emit_qkv(b):
        wb_ = b % NBLK_SEQ
        s0_ = (2 * wb_) % RING
        if b == 0:
            cast_g = 0
            S.wait("tensor", "v", pre_cast0_v)
        else:
            S.wait("gpsimd", "t", proj_last_t.get(b - 1, 0))
            cast_g = S.produce("gpsimd", lambda EE, b=b: EE.dma_start(bfA[:], dx[b]), "gd", 16)
            S.wait("tensor", "gd", cast_g)
        fr = ffn_free.get(b - 1)
        if fr:
            S.wait("tensor", "s", fr[0])
            S.wait("tensor", "v", fr[1])
        PB = [P0, P1]
        bank_user = [("v", stage_v_final) if b == 0 else None, None]
        def bwait(bk):
            if bank_user[bk]:
                S.wait("tensor", bank_user[bk][0], bank_user[bk][1])
        for ec in range(8):
            bk = ec % 2
            if b == 0 and ec == 0:
                S.wait("tensor", "gd", t_part[0])
            bwait(bk)
            for dc in range(8):
                S.produce("tensor", lambda EE, ec=ec, dc=dc, bk=bk: EE.matmul(
                    PB[bk][:, 0:256],
                    wqkv[:, dc * 3072 + ec * 128: dc * 3072 + ec * 128 + 128],
                    bfA[:, dc * 256:(dc + 1) * 256],
                    start=(dc == 0), stop=(dc == 7)), "t", 1)
            S.wait("scalar", "t", S.c("t"))
            bank_user[bk] = ("s", S.produce("scalar", lambda EE, ec=ec, bk=bk: EE.copy(
                qsb[:, ec * 256:(ec + 1) * 256], PB[bk][:, 0:256]), "s", 1))
        for hc in range(8):
            bk = hc % 2
            if b == 0 and hc == 0:
                S.wait("tensor", "gd", t_part[1])
            bwait(bk)
            for dc in range(8):
                S.produce("tensor", lambda EE, hc=hc, dc=dc, bk=bk: EE.matmul(
                    PB[bk][:, 0:256],
                    wqkv[:, dc * 3072 + 1024 + hc * 128: dc * 3072 + 1024 + hc * 128 + 128],
                    bfA[:, dc * 256:(dc + 1) * 256],
                    start=(dc == 0), stop=(dc == 7)), "t", 1)
            S.wait("scalar", "t", S.c("t"))
            bank_user[bk] = ("s", S.produce("scalar", lambda EE, hc=hc, bk=bk, s0_=s0_: EE.copy(
                kring[:, hc * 768 + s0_ * 128: hc * 768 + s0_ * 128 + 256],
                PB[bk][:, 0:256]), "s", 1))
        for j in range(4):
            tsub, vh = j // 2, j % 2
            bk = j % 2
            vslot = s0_ + tsub
            if b == 0 and j == 0:
                S.wait("tensor", "gd", t_part[2])
            bwait(bk)
            for dc in range(8):
                S.produce("tensor", lambda EE, tsub=tsub, vh=vh, dc=dc, bk=bk: EE.matmul(
                    PB[bk][:, 0:512],
                    bfA[:, dc * 256 + tsub * 128: dc * 256 + tsub * 128 + 128],
                    wqkv[:, dc * 3072 + 2048 + vh * 512: dc * 3072 + 2048 + vh * 512 + 512],
                    start=(dc == 0), stop=(dc == 7)), "t", 1)
            S.wait("vector", "t", S.c("t"))
            bank_user[bk] = ("v", S.produce("vector", lambda EE, vslot=vslot, vh=vh, bk=bk: EE.tensor_copy(
                bass.AP(vring, vslot * 1040 + vh * 520,
                        [[RING * 1040, 128], [65, 8], [1, 64]]),
                bass.AP(PB[bk], 0, [[512, 128], [64, 8], [1, 64]])), "v", 1))
        if wb_ == 0:
            S.wait("vector", "s", S.c("s"))
            S.produce("vector", lambda EE: EE.tensor_copy(
                bass.AP(kpers, 0, [[8 * NP, 128], [NP, 8], [1, NP]]),
                bass.AP(kring, 0, [[8 * RING * 128, 128], [768, 8], [1, NP]])), "v", 1)
            vp_v = S.produce("vector", lambda EE: EE.tensor_copy(
                vpers[0:NP, :], vring[0:NP, 0:1040]), "v", 1)
            S.wait("sync", "v", vp_v)
            S.wait("sync", "t", persB_users[0])
            kpersB_x[0] = S.produce("sync", lambda EE: EE.dma_start(
                vpers[32:48, :], vring[0:NP, 0:1040]), "xd", 16)
        qkv_done[b] = (cast_g, S.c("s"), S.c("v"))

    kpersB_x = [0]
    persB_users = [0]
    persexp_done = {}
    emit_qkv(0)
    attn_prev_muls = [0]
    prev_ln3 = [0]

    out_dma = {}
    pair_free = [0, 0]

    for b in range(0 if startup_only else nblk_run):
        wb = b % NBLK_SEQ
        CH = chunk_info[wb]
        nch = len(CH)
        s0 = (2 * wb) % RING
        use_pers = (wb >= 3) and pers_mode >= 2
        use_pers_sc = (wb >= 3) and pers_mode >= 1

        sq_guard = {}
        sq_ready = {}

        def ln_sq(dc, src):
            sq_half = dc % 2
            g = sq_guard.get(sq_half, 0)
            if g:
                S.wait("vector", "t", g)
            sq_ready[dc] = S.produce("vector", lambda EE, dc=dc, sq_half=sq_half: EE.tensor_mul(
                sqb[:, sq_half * 256:sq_half * 256 + 256],
                src(dc), src(dc)), "v", 1)

        def ln_mm(dc, src, bank, mean16=None):
            sq_half = dc % 2
            S.wait("tensor", "v", sq_ready[dc])
            if mean16 is not None:
                S.produce("tensor", lambda EE, dc=dc, mean16=mean16: EE.matmul(
                    bank[0:1, 0:256], onescol16[:],
                    mean16[:, dc * 256:(dc + 1) * 256],
                    start=(dc == 0), stop=False), "t", 1)
            else:
                S.produce("tensor", lambda EE, dc=dc: EE.matmul(
                    bank[0:1, 0:256], onescol[:], src(dc),
                    start=(dc == 0), stop=False), "t", 1)
            sq_guard[sq_half] = S.produce("tensor", lambda EE, dc=dc, sq_half=sq_half: EE.matmul(
                bank[0:1, 256:512], onescol16[:],
                sqb[:, sq_half * 256:sq_half * 256 + 256],
                start=False, stop=(dc == 7)), "t", 1)

        def ln_finish(src, dst, bank, bcb, wide_src, wide_dst, pre_bc_wait=0, pre_bc_wait_v=0):
            S.wait("vector", "t", S.c("t"))
            S.produce("vector", lambda EE: EE.tensor_copy(den16[0:1, 0:256], bank[0:1, 0:256]), "v", 1)
            S.produce("vector", lambda EE: EE.tensor_mul(
                scrF[0:1, 256:512], den16[0:1, 0:256], den16[0:1, 0:256]), "v", 1)
            S.produce("vector", lambda EE: EE.tensor_sub(
                scrF[0:1, 256:512], bank[0:1, 256:512], scrF[0:1, 256:512]), "v", 1)
            var_v = S.produce("vector", lambda EE: EE.tensor_scalar_add(
                scrF[0:1, 256:512], scrF[0:1, 256:512], LN_EPS), "v", 1)
            S.wait("scalar", "v", var_v)
            S.produce("scalar", lambda EE: EE.activation(
                scrF[0:1, 256:512], scrF[0:1, 256:512], AF.Ln), "s", 1)
            r_s = S.produce("scalar", lambda EE: EE.activation(
                den16[0:1, 256:512], scrF[0:1, 256:512], AF.Exp, scale=-0.5), "s", 1)
            S.wait("tensor", "s", r_s)
            S.wait("tensor", "s", pre_bc_wait)
            S.wait("tensor", "v", pre_bc_wait_v)
            bc2 = S.produce("tensor", lambda EE: EE.matmul(
                bcb[:, 0:512], onesr16[0:1, 0:128], den16[0:1, 0:512],
                start=True, stop=True), "t", 1)
            S.wait("vector", "t", bc2)
            wsrc, wdst = wide_src, wide_dst
            halves = []
            for hf in range(2):
                S.produce("vector", lambda EE, hf=hf: EE.tensor_sub(
                    wdst[:, hf * 1024:(hf + 1) * 1024], wsrc[:, hf * 1024:(hf + 1) * 1024],
                    bass.AP(bcb, 0, [[512, 128], [0, 4], [1, 256]])), "v", 1)
                halves.append(S.produce("vector", lambda EE, hf=hf: EE.tensor_mul(
                    wdst[:, hf * 1024:(hf + 1) * 1024], wdst[:, hf * 1024:(hf + 1) * 1024],
                    bass.AP(bcb, 256, [[512, 128], [0, 4], [1, 256]])), "v", 1))
            ln_finish.halves = halves
            return S.c("v")


        if b not in qkv_done:
            emit_qkv(b)
        cast_g, qkv_s, qkv_vv = qkv_done[b]
        S.wait("tensor", "s", qkv_s)
        S.wait("tensor", "v", qkv_vv)

        if stage < 2:
            continue

        pairs = [(P2, P3), (P6, P7)]
        units = []
        for hp in range(8):
            ci = 0
            first = True
            for ui, i in enumerate(range(0, nch, 2)):
                grp = CH[i:i + 2]
                units.append((hp, grp, ci, first, i + len(grp) >= nch,
                              unit_mask_idx[(wb, ui)]))
                ci += len(grp)
                first = False
        NU = len(units)
        unit_tmm = {}
        unit_exp = {}
        unit_mask = {}
        unit_av = {}
        persexp_done = {}
        evac_done = {}
        muls_done = {-1: attn_prev_muls[0], -2: attn_prev_muls[0]}
        rcp_done = {}
        cp_done = {}
        bc_done = {}
        stagB_x = {}
        avflag = {}

        def sc_unit(k):
            hp, grp, ci, first, last, umi = units[k]
            pA_, pB_ = pairs[k % 2]
            if k >= 2:
                S.wait("tensor", "s", unit_exp.get(k - 2, 0))
            elif pair_free[k % 2]:
                S.wait("tensor", "v", pair_free[k % 2])
            for j, (gc, sl, mi) in enumerate(grp):
                S.produce("tensor", lambda EE, sl=sl, hp=hp, j=j, pA_=pA_: EE.matmul(
                    pA_[:, j * 256:j * 256 + 256],
                    kring[0:64, hp * 768 + sl * 128: hp * 768 + sl * 128 + 128],
                    qsb[0:64, hp * 256:hp * 256 + 256],
                    start=True, stop=True), "t", 1)
                S.produce("tensor", lambda EE, sl=sl, hp=hp, j=j, pB_=pB_: EE.matmul(
                    pB_[:, j * 256:j * 256 + 256],
                    kring[64:128, hp * 768 + sl * 128: hp * 768 + sl * 128 + 128],
                    qsb[64:128, hp * 256:hp * 256 + 256],
                    start=True, stop=True), "t", 1)
            unit_tmm[k] = S.c("t")

        last_persexp = [0]

        def pers_mm(hp):
            S.wait("tensor", "v", muls_done.get(hp - 1, 0))
            S.wait("tensor", "s", qkv_s)
            S.wait("tensor", "v", max(qkv_vv, prev_ln3[0]))
            S.wait("tensor", "xd", kpersB_x[0])
            for sub in range(2):
                h2 = hp + sub
                S.produce("tensor", lambda EE, h2=h2, sub=sub: EE.matmul(
                    P0[0:NP, sub * 256:sub * 256 + 256],
                    kpers[0:64, h2 * NP: h2 * NP + NP],
                    qsb[0:64, h2 * 256:h2 * 256 + 256],
                    start=True, stop=True), "t", 1)
            for sub in range(2):
                h2 = hp + sub
                pt_t = S.produce("tensor", lambda EE, h2=h2, sub=sub: EE.matmul(
                    P1[32:32 + NP, sub * 256:sub * 256 + 256],
                    kpers[64:128, h2 * NP: h2 * NP + NP],
                    qsb[64:128, h2 * 256:h2 * 256 + 256],
                    start=True, stop=True), "t", 1)
            S.wait("scalar", "t", pt_t)
            S.produce("scalar", lambda EE: EE.activation(
                ptP[0:NP, 0:512], P0[0:NP, 0:512],
                AF.Exp, scale=SCALE), "s", 1)
            pe_s = S.produce("scalar", lambda EE: EE.activation(
                ptP[32:32 + NP, 0:512], P1[32:32 + NP, 0:512],
                AF.Exp, scale=SCALE), "s", 1)
            persexp_done[hp] = pe_s
            persexp_done[hp + 1] = pe_s
            last_persexp[0] = pe_s
            persB_users[0] = pe_s

        def exp_unit(k):
            hp, grp, ci, first, last, umi = units[k]
            pA_, pB_ = pairs[k % 2]
            gl = len(grp)
            so = (k % 3) * 512
            S.wait("scalar", "t", unit_tmm[k])
            S.wait("scalar", "t", unit_av.get(k - 3, 0))
            S.produce("scalar", lambda EE, so=so, gl=gl, pA_=pA_: EE.activation(
                ptA[:, so: so + gl * 256], pA_[:, 0:gl * 256],
                AF.Exp, scale=SCALE), "s", 1)
            unit_exp[k] = S.produce("scalar", lambda EE, so=so, gl=gl, pB_=pB_: EE.activation(
                ptB[:, so: so + gl * 256], pB_[:, 0:gl * 256],
                AF.Exp, scale=SCALE), "s", 1)
            if umi is not None:
                S.wait("vector", "s", unit_exp[k])
                w = gl * 256
                for pt in (ptA, ptB):
                    S.produce("vector", lambda EE, pt=pt, so=so, w=w, umi=umi: EE.tensor_mul(
                        pt[:, so: so + w],
                        pt[:, so: so + w],
                        maskt[:, umi * 512: umi * 512 + w]), "v", 1)
            unit_mask[k] = S.c("v") if umi is not None else 0

        def av_unit(k):
            hp, grp, ci, first, last, umi = units[k]
            so = (k % 3) * 512
            S.wait("tensor", "s", unit_exp[k])
            if last and use_pers:
                S.wait("tensor", "s", persexp_done[hp])
            S.wait("tensor", "v", unit_mask[k])
            if first:
                S.wait("tensor", "v", evac_done.get(hp - 1, 0))
                avflag[hp] = True
            for j, (gc, sl, mi) in enumerate(grp):
                st = avflag[hp]
                avflag[hp] = False
                fin = last and (j == len(grp) - 1) and not use_pers
                S.produce("tensor", lambda EE, sl=sl, hp=hp, so=so, j=j, st=st, fin=fin: EE.matmul(
                    P4[0:65, 0:256],
                    vring[:, sl * 1040 + 2 * hp * 65: sl * 1040 + 2 * hp * 65 + 65],
                    ptA[:, so + j * 256: so + j * 256 + 256],
                    start=st, stop=fin), "t", 1)
                S.produce("tensor", lambda EE, sl=sl, hp=hp, so=so, j=j, st=st, fin=fin: EE.matmul(
                    P5[0:65, 0:256],
                    vring[:, sl * 1040 + (2 * hp + 1) * 65: sl * 1040 + (2 * hp + 1) * 65 + 65],
                    ptB[:, so + j * 256: so + j * 256 + 256],
                    start=st, stop=fin), "t", 1)
            if last and use_pers:
                po = (hp % 2) * 256
                S.produce("tensor", lambda EE, hp=hp, po=po: EE.matmul(
                    P4[0:65, 0:256], vpers[0:NP, 2 * hp * 65: 2 * hp * 65 + 65],
                    ptP[0:NP, po:po + 256], start=False, stop=True), "t", 1)
                S.produce("tensor", lambda EE, hp=hp, po=po: EE.matmul(
                    P5[0:65, 0:256],
                    vpers[32:32 + NP, (2 * hp + 1) * 65: (2 * hp + 1) * 65 + 65],
                    ptP[32:32 + NP, po:po + 256], start=False, stop=True), "t", 1)
            unit_av[k] = S.c("t")
            if last:
                db = den if hp % 2 == 0 else scrF
                S.wait("vector", "t", unit_av[k])
                S.wait("vector", "s", rcp_done.get(hp - 2, 0))
                S.produce("vector", lambda EE, db=db: EE.tensor_copy(
                    db[64:65, 0:256], P4[64:65, 0:256]), "v", 1)
                cp_v = S.produce("vector", lambda EE, db=db: EE.tensor_copy(
                    db[64:65, 256:512], P5[64:65, 0:256]), "v", 1)
                cp_done[hp] = cp_v
                S.produce("vector", lambda EE, hp=hp: EE.tensor_copy(
                    bfA[0:64, hp * 256:hp * 256 + 256], P4[0:64, 0:256]), "v", 1)
                evac_done[hp] = S.produce("vector", lambda EE, hp=hp: EE.tensor_copy(
                    xhat[0:64, hp * 256:hp * 256 + 256], P5[0:64, 0:256]), "v", 1)

        def tail_bc(hp):
            db = den if hp % 2 == 0 else scrF
            PB_ = P0 if hp < 6 else (P6 if hp == 6 else P7)
            S.wait("scalar", "v", cp_done[hp])
            S.wait("scalar", "t", bc_done.get(hp - 1, 0))
            S.produce("scalar", lambda EE, db=db: EE.activation(
                db[64:65, 0:512], db[64:65, 0:512], AF.Ln), "s", 1)
            rcp_done[hp] = S.produce("scalar", lambda EE, db=db: EE.activation(
                den16[64:65, 0:512], db[64:65, 0:512], AF.Exp, scale=-1.0), "s", 1)
            S.wait("tensor", "s", rcp_done[hp])
            if hp < 6:
                S.wait("tensor", "s", last_persexp[0])
                S.wait("tensor", "v", muls_done.get(hp - 1, 0))
            bc_t = S.produce("tensor", lambda EE, PB_=PB_: EE.matmul(
                PB_[:, 0:512], onesr16[64:65, 0:128], den16[64:65, 0:512],
                start=True, stop=True), "t", 1)
            bc_done[hp] = bc_t
            S.wait("vector", "t", bc_t)
            S.produce("vector", lambda EE, hp=hp, PB_=PB_: EE.tensor_mul(
                bfA[0:64, hp * 256:hp * 256 + 256],
                bfA[0:64, hp * 256:hp * 256 + 256], PB_[0:64, 0:256]), "v", 1)
            muls_done[hp] = S.produce("vector", lambda EE, hp=hp, PB_=PB_: EE.tensor_mul(
                xhat[0:64, hp * 256:hp * 256 + 256],
                xhat[0:64, hp * 256:hp * 256 + 256], PB_[0:64, 256:512]), "v", 1)
            S.wait("sync", "v", muls_done[hp])
            stagB_x[hp] = S.produce("sync", lambda EE, hp=hp: EE.dma_start(
                bfA[64:128, hp * 256:hp * 256 + 256],
                xhat[0:64, hp * 256:hp * 256 + 256]), "xd", 16)

        pend_q = []
        for k in range(NU):
            hp, grp, ci, first, last, umi = units[k]
            sc_unit(k)
            if k >= 1:
                av_unit(k - 1)
            if pend_q and k >= pend_q[0][1] + 3:
                tail_bc(pend_q.pop(0)[0])
            if first and use_pers_sc and hp % 2 == 0:
                pers_mm(hp)
            exp_unit(k)
            if k >= 1 and units[k - 1][4]:
                pend_q.append((units[k - 1][0], k - 1))
        av_unit(NU - 1)
        for hp_, _ in pend_q:
            tail_bc(hp_)
        tail_bc(7)
        attn_prev_muls[0] = muls_done[7]
        attn_v = S.c("v")
        if stage < 3:
            continue

        if dbg:
            S.wait("gpsimd", "xd", stagB_x[7])
            S.produce("gpsimd", lambda EE, b=b: EE.dma_start(dbg_attn[b], bfA[:]), "gd", 16)
            S.wait("gpsimd", "v", S.c("v"))
            S.produce("gpsimd", lambda EE, b=b: EE.dma_start(dbg_q[b], qsb[:]), "gd", 16)

        if b == 0:
            S.wait("tensor", "gd", t_wout)
        S.wait("tensor", "v", evac_done[7])
        srcz = lambda dc: xslot[:, dc * 256: dc * 256 + 256]
        S.wait("vector", "xd", x_ready.get(b, x_ready[0]))
        S.wait("tensor", "v", prev_ln3[0])
        PBp = [P0, P1]
        pbank_user = [("v", muls_done[5]), ("s", persexp_done.get(7, 0))]
        for ep in range(4):
            bk = ep % 2
            if pbank_user[bk] and pbank_user[bk][1]:
                S.wait("tensor", pbank_user[bk][0], pbank_user[bk][1])
            for half in range(2):
                ec = 2 * ep + half
                for dc in range(8):
                    if ep == 0 and half == 0:
                        S.wait("tensor", "xd", stagB_x[dc])
                    S.produce("tensor", lambda EE, ec=ec, dc=dc, bk=bk, half=half: EE.matmul(
                        PBp[bk][:, half * 256:half * 256 + 256],
                        wout[:, dc * 1024 + ec * 128: dc * 1024 + ec * 128 + 128],
                        bfA[:, dc * 256:(dc + 1) * 256],
                        start=(dc == 0), stop=(dc == 7)), "t", 1)
            S.wait("vector", "t", S.c("t"))
            pbank_user[bk] = ("v", S.produce("vector", lambda EE, ep=ep, bk=bk: EE.tensor_add(
                xslot[:, ep * 512: ep * 512 + 512],
                xslot[:, ep * 512: ep * 512 + 512],
                PBp[bk][:, 0:512]), "v", 1))
            for ec in (2 * ep, 2 * ep + 1):
                if ec >= 2:
                    ln_mm(ec - 2, srcz, P4)
                ln_sq(ec, srcz)
        ln_mm(6, srcz, P4)
        ln_mm(7, srcz, P4)
        proj_last_t[b] = S.c("t")

        if stage < 4:
            continue

        if b - 1 in out_dma:
            S.wait("vector", "gd", out_dma[b - 1])
        S.wait("vector", "xd", stagB_x[7])
        srcx1 = lambda dc: x1slot[:, dc * 256:(dc + 1) * 256]
        S.wait("vector", "t", S.c("t"))
        S.produce("vector", lambda EE: EE.tensor_copy(den16[0:1, 0:256], P4[0:1, 0:256]), "v", 1)
        S.produce("vector", lambda EE: EE.tensor_mul(
            scrF[0:1, 256:512], den16[0:1, 0:256], den16[0:1, 0:256]), "v", 1)
        S.produce("vector", lambda EE: EE.tensor_sub(
            scrF[0:1, 256:512], P4[0:1, 256:512], scrF[0:1, 256:512]), "v", 1)
        var_v = S.produce("vector", lambda EE: EE.tensor_scalar_add(
            scrF[0:1, 256:512], scrF[0:1, 256:512], LN_EPS), "v", 1)
        S.wait("scalar", "v", var_v)
        S.produce("scalar", lambda EE: EE.activation(
            scrF[0:1, 256:512], scrF[0:1, 256:512], AF.Ln), "s", 1)
        r_s1 = S.produce("scalar", lambda EE: EE.activation(
            den16[0:1, 256:512], scrF[0:1, 256:512], AF.Exp, scale=-0.5), "s", 1)
        S.wait("tensor", "s", r_s1)
        S.wait("tensor", "v", evac_done[7])
        bc1 = S.produce("tensor", lambda EE: EE.matmul(
            P2[:, 0:512], onesr16[0:1, 0:128], den16[0:1, 0:512],
            start=True, stop=True), "t", 1)
        S.wait("vector", "t", bc1)
        half_v = {}
        for hf in range(2):
            S.produce("vector", lambda EE, hf=hf: EE.tensor_sub(
                x1slot[:, hf * 1024:(hf + 1) * 1024], xslot[:, hf * 1024:(hf + 1) * 1024],
                bass.AP(P2, 0, [[512, 128], [0, 4], [1, 256]])), "v", 1)
            half_v[hf] = S.produce("vector", lambda EE, hf=hf: EE.tensor_mul(
                x1slot[:, hf * 1024:(hf + 1) * 1024], x1slot[:, hf * 1024:(hf + 1) * 1024],
                bass.AP(P2, 256, [[512, 128], [0, 4], [1, 256]])), "v", 1)
        wide1_v = half_v[1]
        gate_started = [False]

        def gate_mms(dc):
            S.produce("tensor", lambda EE, dc=dc: EE.matmul(
                P3[0:8, 0:256],
                wgf[:, dc * 8:(dc + 1) * 8],
                x1slot[:, dc * 256:(dc + 1) * 256],
                start=(dc == 0), stop=(dc == 7)), "t", 1)

        for dc in range(8):
            if dc == 0:
                S.wait("tensor", "v", half_v[0])
            if dc == 4:
                S.wait("tensor", "v", half_v[1])
            if dc >= 2:
                ln_mm(dc - 2, srcx1, P5)
                gate_mms(dc - 2)
            ln_sq(dc, srcx1)
        for dc in (6, 7):
            ln_mm(dc, srcx1, P5)
            gate_mms(dc)
        ln1_v = S.c("v")

        if b + 1 < NBLK:
            S.wait("sync", "v", ln1_v)
            x_ready[b + 1] = S.produce("sync", lambda EE, b=b: EE.dma_start(
                xslot[:], dx[b + 1]), "xd", 16)

        if dbg:
            S.wait("gpsimd", "v", ln1_v)
            S.produce("gpsimd", lambda EE, b=b: EE.dma_start(dbg_x1[b], x1slot[:]), "gd", 16)
        if stage < 5:
            continue

        ln2_v = ln_finish(srcx1,
                          lambda dc: xhat[:, dc * 256:(dc + 1) * 256], P5, P2,
                          x1slot, xhat)

        if dbg:
            S.wait("gpsimd", "v", ln2_v)
            S.produce("gpsimd", lambda EE, b=b: EE.dma_start(dbg_xh[b], xhat[:]), "gd", 16)

        gate_t = S.c("t")
        gate_done[b] = gate_t
        S.wait("vector", "t", gate_t)
        lg_v = S.produce("vector", lambda EE: EE.tensor_copy(
            scrF[0:8, 0:256], P3[0:8, 0:256]), "v", 1)
        S.wait("tensor", "v", lg_v)
        for tsub in range(2):
            S.produce("tensor", lambda EE, tsub=tsub: EE.transpose(
                P5[:, tsub * 8:tsub * 8 + 8],
                scrF[0:8, tsub * 128:tsub * 128 + 128], ident[0:8, 0:8]), "t", 1)
        lgT_t = S.c("t")
        S.wait("vector", "t", lgT_t)
        for tsub in range(2):
            rmx = S.produce("vector", lambda EE, tsub=tsub: EE.reduce_max(
                maxsb[:, tsub:tsub + 1], P5[:, tsub * 8:tsub * 8 + 8], axis=AX.X), "v", 1)
        S.wait("vector", "v", rmx)
        for tsub in range(2):
            S.produce("vector", lambda EE, tsub=tsub: EE.tensor_scalar(
                msksb[:, tsub * 8:tsub * 8 + 8], P5[:, tsub * 8:tsub * 8 + 8],
                maxsb[:, tsub:tsub + 1], None, ALU.is_ge), "v", 1)
        msk_v = S.c("v")
        if dbg:
            S.wait("gpsimd", "v", msk_v)
            S.produce("gpsimd", lambda EE, b=b: EE.dma_start(dbg_mx[b], maxsb[:]), "gd", 16)
        S.wait("tensor", "v", msk_v)
        for tsub in range(2):
            S.produce("tensor", lambda EE, tsub=tsub: EE.transpose(
                P3[0:8, tsub * 128:tsub * 128 + 128],
                msksb[:, tsub * 8:tsub * 8 + 8], ident[:]), "t", 1)
        mskT_t = S.c("t")
        S.wait("vector", "t", mskT_t)
        mskT_v = S.produce("vector", lambda EE: EE.tensor_copy(
            mskT16[0:8, 0:256], P3[0:8, 0:256]), "v", 1)
        if dbg:
            S.wait("gpsimd", "v", mskT_v)
            S.produce("gpsimd", lambda EE, b=b: EE.dma_start(dbg_msk[b], msksb[:]), "gd", 16)
            S.produce("gpsimd", lambda EE, b=b: EE.dma_start(dbg_mt[b], mskT16[0:8, 0:256]), "gd", 16)

        if stage < 7:
            continue

        HB = [P0, P7]
        gel_of = {}
        hm_of = {}
        first_y = [True] * 4

        def w1_mms(e):
            hb = HB[e % 2]
            S.wait("tensor", "s", gel_of.get(e - 2, 0))
            for hc in range(2):
                for dc in range(8):
                    if e == 0 and hc == 0 and dc in (0, 4):
                        S.wait("tensor", "v", ln_finish.halves[dc // 4])
                    S.produce("tensor", lambda EE, e=e, hc=hc, dc=dc, hb=hb: EE.matmul(
                        hb[:, hc * 256:hc * 256 + 256],
                        w1p[:, (e * 8 + dc) * HF + hc * 128:
                            (e * 8 + dc) * HF + hc * 128 + 128],
                        xhat[:, dc * 256:(dc + 1) * 256],
                        start=(dc == 0), stop=(dc == 7)), "t", 1)
            return S.c("t")

        def gelu_hm(e, h_t):
            hb = HB[e % 2]
            S.wait("tensor", "v", mskT_v if e == 0 else S.c("v"))
            bce_t = S.produce("tensor", lambda EE, e=e: EE.matmul(
                P1[:, 0:256], e8f[0:8, e * 128:(e + 1) * 128], mskT16[0:8, 0:256],
                start=True, stop=True), "t", 1)
            S.wait("scalar", "t", h_t)
            S.wait("scalar", "v", hm_of.get(e - 2, 0))
            hs = Hsb2 if e % 2 else Hsb
            for hc in range(2):
                S.produce("scalar", lambda EE, e=e, hc=hc, hb=hb, hs=hs: EE.activation(
                    hs[:, hc * 256:hc * 256 + 256], hb[:, hc * 256:hc * 256 + 256],
                    AF.Gelu, bias=b1tot[:, e * 2 + hc: e * 2 + hc + 1]), "s", 1)
            gel_of[e] = S.c("s")
            S.wait("vector", "s", gel_of[e])
            S.wait("vector", "t", bce_t)
            hm_of[e] = S.produce("vector", lambda EE, hs=hs: EE.tensor_mul(
                hs[:, 0:512], hs[:, 0:512],
                bass.AP(P1, 0, [[512, 128], [0, 2], [1, 256]])), "v", 1)

        def w2_mms(e):
            hs = Hsb2 if e % 2 else Hsb
            if b == 0 and e == 0:
                S.wait("tensor", "gd", t_w2)
            S.wait("tensor", "v", hm_of[e])
            for dc in range(8):
                yb = PY[dc // 2]
                yoff = (dc % 2) * 256
                for hc in range(2):
                    st = first_y[dc // 2]
                    first_y[dc // 2] = False
                    S.produce("tensor",
                              lambda EE, e=e, hc=hc, dc=dc, yb=yb, yoff=yoff, st=st, hs=hs: EE.matmul(
                                  yb[:, yoff:yoff + 256],
                                  w2p[:, (e * 2 + hc) * 1024 + dc * 128:
                                      (e * 2 + hc) * 1024 + dc * 128 + 128],
                                  hs[:, hc * 256:hc * 256 + 256],
                                  start=st, stop=False), "t", 1)

        ht = w1_mms(0)
        gelu_hm(0, ht)
        for e in range(1, E):
            ht = w1_mms(e)
            w2_mms(e - 1)
            gelu_hm(e, ht)
        w2_mms(E - 1)
        for dc in range(8):
            yb = PY[dc // 2]
            yoff = (dc % 2) * 256
            S.produce("tensor", lambda EE, dc=dc, yb=yb, yoff=yoff: EE.matmul(
                yb[:, yoff:yoff + 256],
                b2s[0:8, dc * 128:(dc + 1) * 128], mskT16[0:8, 0:256],
                start=False, stop=True), "t", 1)
        y_t = S.c("t")
        ffn_free[b] = (S.c("s"), S.c("v"))
        if b + 1 < nblk_run:
            emit_qkv(b + 1)
        S.wait("vector", "t", y_t)
        S.wait("tensor", "s", S.c("s"))
        for pp_ in range(4):
            S.produce("vector", lambda EE, pp_=pp_: EE.tensor_add(
                x1slot[:, pp_ * 512:(pp_ + 1) * 512],
                x1slot[:, pp_ * 512:(pp_ + 1) * 512], PY[pp_][:, 0:512]), "v", 1)
            for dc in (2 * pp_, 2 * pp_ + 1):
                if dc >= 2:
                    ln_mm(dc - 2, srcx1, P6)
                ln_sq(dc, srcx1)
        ln_mm(6, srcx1, P6)
        ln_mm(7, srcx1, P6)

        if dbg:
            S.wait("gpsimd", "v", S.c("v"))
            dmp2 = S.produce("gpsimd", lambda EE, b=b: EE.dma_start(dbg_z2[b], x1slot[:]), "gd", 16)
            S.wait("vector", "gd", dmp2)

        z2add_v = S.c("v")
        ln3_v = ln_finish(srcx1, srcx1, P6, P1, x1slot, x1slot,
                          pre_bc_wait=qkv_done[b + 1][1] if b + 1 in qkv_done else 0,
                          pre_bc_wait_v=qkv_done[b + 1][2] if b + 1 in qkv_done else 0)
        pair_free[0] = z2add_v
        pair_free[1] = ln3_v
        prev_ln3[0] = ln3_v

        S.wait("gpsimd", "v", ln3_v)
        out_dma[b] = S.produce("gpsimd", lambda EE, b=b: EE.dma_start(
            dy[b], x1slot[:]), "gd", 16)

    S.run("gpsimd", lambda EE: EE.wait_ge(sems["gd"], S.c("gd")))

    with nc.Block() as block:
        @block.sync
        def _(EE):
            for fn in S.ops["sync"]:
                fn(EE)

        @block.gpsimd
        def _(EE):
            for fn in S.ops["gpsimd"]:
                fn(EE)

        @block.vector
        def _(EE):
            for fn in S.ops["vector"]:
                fn(EE)

        @block.scalar
        def _(EE):
            for fn in S.ops["scalar"]:
                fn(EE)

        @block.tensor
        def _(EE):
            for fn in S.ops["tensor"]:
                fn(EE)

    build_program._last_S = S
    return nc, es


_CACHE = {}


def kernel(L0_attn_x, L0_attn_W_qkv, L0_attn_W_out, L0_ffn_Wg, L0_ffn_ln_scale,
           L0_ffn_ln_bias, L0_ffn_W1, L0_ffn_b1, L0_ffn_W2, L0_ffn_b2):
    x = np.ascontiguousarray(np.asarray(L0_attn_x, dtype=np.float32))
    Wqkv = np.ascontiguousarray(np.asarray(L0_attn_W_qkv, dtype=np.float32))
    Wout = np.ascontiguousarray(np.asarray(L0_attn_W_out, dtype=np.float32))
    Wg = np.ascontiguousarray(np.asarray(L0_ffn_Wg, dtype=np.float32))
    lns = np.ascontiguousarray(np.asarray(L0_ffn_ln_scale, dtype=np.float32))
    lnb = np.ascontiguousarray(np.asarray(L0_ffn_ln_bias, dtype=np.float32))
    W1 = np.ascontiguousarray(np.asarray(L0_ffn_W1, dtype=np.float32))
    b1 = np.ascontiguousarray(np.asarray(L0_ffn_b1, dtype=np.float32))
    W2 = np.ascontiguousarray(np.asarray(L0_ffn_W2, dtype=np.float32))
    b2 = np.ascontiguousarray(np.asarray(L0_ffn_b2, dtype=np.float32))

    mask_tiles, mask_map = _masks_for_blocks()

    if "prog" not in _CACHE:
        _CACHE["prog"] = build_program(mask_tiles, mask_map)
    nc, _es = _CACHE["prog"]

    wqkvT = np.ascontiguousarray(Wqkv.T.reshape(8, 128, 3072))
    woutT = np.ascontiguousarray(Wout.T.reshape(8, 128, 1024))
    w1h = np.ascontiguousarray(W1.reshape(E, 8, 128, HF))
    w2h = np.ascontiguousarray(W2.reshape(E, 2, 128, 1024))
    wgT = np.ascontiguousarray(Wg.T.reshape(8, 128, E))
    lnsh = np.ascontiguousarray(lns.reshape(E, 8, 128).transpose(2, 0, 1))
    lnbh = np.ascontiguousarray(lnb.reshape(E, 8, 128).transpose(2, 0, 1))
    b1h = np.ascontiguousarray(b1.reshape(E, 2, 128).transpose(2, 0, 1))
    maskh = np.ascontiguousarray(mask_tiles.astype(BF))
    identh = np.eye(128, dtype=np.float32)
    e8h = np.zeros((8, E, 128), dtype=np.float32)
    for e in range(E):
        e8h[e, e, :] = 1.0

    in_maps = []
    for c in range(NCORES):
        xc = x[SEQ_PER_CORE * c: SEQ_PER_CORE * (c + 1)].reshape(SEQ_PER_CORE * T, D)
        xfm = np.ascontiguousarray(xc.T)
        x5 = np.ascontiguousarray(
            xfm.reshape(8, 128, NBLK, BLK).transpose(2, 1, 0, 3))
        in_maps.append({
            "x5": x5, "wqkvT": wqkvT, "woutT": woutT, "w1h": w1h, "w2h": w2h,
            "wgT": wgT, "lnsh": lnsh, "lnbh": lnbh, "b1h": b1h, "b2h": b2,
            "maskh": maskh, "identh": identh, "e8h": e8h,
        })

    res = run_bass_kernel_spmd(nc, in_maps, core_ids=list(range(NCORES)))

    out = np.empty((NCORES * SEQ_PER_CORE, T, D), dtype=np.float32)
    for c in range(NCORES):
        y5 = np.asarray(res.results[c]["y5"])
        yfm = y5.transpose(2, 1, 0, 3).reshape(D, SEQ_PER_CORE * T)
        out[SEQ_PER_CORE * c: SEQ_PER_CORE * (c + 1)] = \
            yfm.T.reshape(SEQ_PER_CORE, T, D)
    return out


# version 42
# speedup vs baseline: 1.0328x; 1.0328x over previous
import numpy as np
import ml_dtypes
from contextlib import ExitStack

import concourse.bass as bass
import concourse.mybir as mybir
from concourse.bass_utils import run_bass_kernel_spmd

F32 = mybir.dt.float32
BF16 = mybir.dt.bfloat16
AF = mybir.ActivationFunctionType
ALU = mybir.AluOpType
AX = mybir.AxisListType
BF = ml_dtypes.bfloat16

D = 1024
T = 2048
NCORES = 8
SEQ_PER_CORE = 2
BLK = 256
NBLK_SEQ = T // BLK
NBLK = SEQ_PER_CORE * NBLK_SEQ
NHEAD = 16
DH = 64
WINDOW = 512
NP = 16
C_EXP = 16.0
SCALE = 0.125
LN_EPS = 1e-5
E = 8
HF = 256
RING = 6

ENGS = ("sync", "gpsimd", "vector", "scalar", "tensor")


def _masks_for_blocks():
    tiles, keys, mapping = [], {}, {}
    for wb in range(NBLK_SEQ):
        gmin = max(0, 2 * wb - 4)
        gcs = list(range(gmin, 2 * wb + 2))
        for ui, i in enumerate(range(0, len(gcs), 2)):
            pair = gcs[i:i + 2]
            cols = []
            for gc in pair:
                k = 128 * gc + np.arange(128)[:, None]
                q = 256 * wb + np.arange(256)[None, :]
                band = (np.abs(q - k) <= WINDOW) & (q >= NP) & (k >= NP) & (q >= k)
                cols.append(((k < NP) | band))
            allowed = np.concatenate(cols, axis=1)
            if allowed.all():
                mapping[(wb, ui)] = None
                continue
            key = allowed.tobytes()
            if key not in keys:
                keys[key] = len(tiles)
                tiles.append(allowed.astype(np.float32))
            mapping[(wb, ui)] = keys[key]
    return np.stack(tiles), mapping


class Sched:
    def __init__(self):
        self.ops = {e: [] for e in ENGS}
        self.cnt = {}
        self.sems = {}
        self.log = {e: [] for e in ENGS}

    def produce(self, eng, fn, sem, inc=1):
        self.cnt[sem] = self.cnt.get(sem, 0) + inc
        tgt = self.cnt[sem]

        def wrapped(EE, fn=fn, sem=sem, inc=inc):
            fn(EE).then_inc(self.sems[sem], inc)
        self.ops[eng].append(wrapped)
        self.log[eng].append(("i", sem, inc))
        return tgt

    def run(self, eng, fn):
        self.ops[eng].append(fn)

    def wait(self, eng, sem, val):
        if val is None or val <= 0:
            return
        have = self.cnt.get(sem, 0)
        assert val <= have, f"wait {eng} {sem} {val} > emitted {have}"
        self.ops[eng].append(lambda EE, sem=sem, val=val: EE.wait_ge(self.sems[sem], val))
        self.log[eng].append(("w", sem, val))

    def c(self, sem):
        return self.cnt.get(sem, 0)

    def simulate(self):
        pos = {e: 0 for e in ENGS}
        val = {}
        progress = True
        while progress:
            progress = False
            for e in ENGS:
                while pos[e] < len(self.log[e]):
                    kind, sem, x = self.log[e][pos[e]]
                    if kind == "i":
                        val[sem] = val.get(sem, 0) + x
                    else:
                        if val.get(sem, 0) < x:
                            break
                    pos[e] += 1
                    progress = True
        stuck = {e: (pos[e], len(self.log[e]), self.log[e][pos[e]] if pos[e] < len(self.log[e]) else None)
                 for e in ENGS if pos[e] < len(self.log[e])}
        return stuck or None


def build_program(mask_tiles, mask_map, nblk_run=NBLK, startup_only=False, stage=99, qkvs=9, dbg=False, pers_mode=2):
    NM = mask_tiles.shape[0]
    nc = bass.Bass("TRN2", target_bir_lowering=False)

    dx = nc.dram_tensor("x5", [NBLK, 128, 8, BLK], F32, kind="ExternalInput").ap()
    dy = nc.dram_tensor("y5", [NBLK, 128, 8, BLK], F32, kind="ExternalOutput").ap()
    dwqkv = nc.dram_tensor("wqkvT", [8, 128, 3072], F32, kind="ExternalInput").ap()
    dwout = nc.dram_tensor("woutT", [8, 128, 1024], F32, kind="ExternalInput").ap()
    dw1 = nc.dram_tensor("w1h", [E, 8, 128, HF], F32, kind="ExternalInput").ap()
    dw2 = nc.dram_tensor("w2h", [E, 2, 128, 1024], F32, kind="ExternalInput").ap()
    dwg = nc.dram_tensor("wgT", [8, 128, E], F32, kind="ExternalInput").ap()
    dlns = nc.dram_tensor("lnsh", [128, E, 8], F32, kind="ExternalInput").ap()
    dlnb = nc.dram_tensor("lnbh", [128, E, 8], F32, kind="ExternalInput").ap()
    db1 = nc.dram_tensor("b1h", [128, E, 2], F32, kind="ExternalInput").ap()
    db2 = nc.dram_tensor("b2h", [E, 1024], F32, kind="ExternalInput").ap()
    dmask = nc.dram_tensor("maskh", [NM, 128, 2 * BLK], BF16, kind="ExternalInput").ap()
    dident = nc.dram_tensor("identh", [128, 128], F32, kind="ExternalInput").ap()
    if dbg:
        dbg_attn = nc.dram_tensor("dbg_attn", [NBLK, 128, 2048], BF16, kind="ExternalOutput").ap()
        dbg_x1 = nc.dram_tensor("dbg_x1", [NBLK, 128, 2048], F32, kind="ExternalOutput").ap()
        dbg_xh = nc.dram_tensor("dbg_xh", [NBLK, 128, 2048], BF16, kind="ExternalOutput").ap()
        dbg_q = nc.dram_tensor("dbg_q", [NBLK, 128, 2048], BF16, kind="ExternalOutput").ap()
        dbg_msk = nc.dram_tensor("dbg_msk", [NBLK, 128, 16], F32, kind="ExternalOutput").ap()
        dbg_mt = nc.dram_tensor("dbg_mt", [NBLK, 8, 256], BF16, kind="ExternalOutput").ap()
        dbg_h = nc.dram_tensor("dbg_h", [NBLK, 128, 512], BF16, kind="ExternalOutput").ap()
        dbg_z2 = nc.dram_tensor("dbg_z2", [NBLK, 128, 2048], F32, kind="ExternalOutput").ap()
        dbg_lg = nc.dram_tensor("dbg_lg", [NBLK, 128, 16], F32, kind="ExternalOutput").ap()
        dbg_mx = nc.dram_tensor("dbg_mx", [NBLK, 128, 2], F32, kind="ExternalOutput").ap()
        dbg_b1 = nc.dram_tensor("dbg_b1", [128, 16], F32, kind="ExternalOutput").ap()
    de8 = nc.dram_tensor("e8h", [8, E, 128], F32, kind="ExternalInput").ap()

    es = ExitStack()

    def sb(name, shape, dt=F32):
        return es.enter_context(nc.sbuf_tensor(name, shape, dt))

    def psmk(name):
        return es.enter_context(nc.psum_tensor(name, [128, 512], F32))

    wqkv = sb("wqkv", [128, 8 * 3072], BF16)
    wout = sb("wout", [128, 8 * 1024], BF16)
    w1p = sb("w1p", [128, E * 8 * HF], BF16)
    w2p = sb("w2p", [128, E * 2 * 1024], BF16)
    lnssb = sb("lnssb", [128, E * 8], F32)
    lnbsb = sb("lnbsb", [128, E * 8], F32)
    b1tot = sb("b1tot", [128, E * 2], F32)
    b2s = sb("b2s", [E, 1024], BF16)
    e8f = sb("e8f", [8, E * 128], BF16)
    maskt = sb("maskt", [128, NM * 2 * BLK], BF16)
    ident = sb("identsb", [128, 128], F32)
    cbias = sb("cbias", [128, 1], F32)
    onescol = sb("onescol", [128, 1], F32)
    onescol16 = sb("onescol16", [128, 1], BF16)
    sqb = sb("sqb", [128, 512], BF16)
    onesr16 = sb("onesr16", [128, 128], BF16)
    den16 = sb("den16", [128, 512], BF16)

    xslot = sb("xslot", [128, 2048], F32)
    bfA = sb("bfA", [128, 2048], BF16)
    qsb = sb("qsb", [128, 8 * BLK], BF16)
    kring = sb("kringsb", [128, 8 * RING * 128], BF16)
    kpers = sb("kpers", [128, 8 * NP], BF16)
    vring = sb("vringsb", [128, RING * 1040], BF16)
    vpers = sb("vpers", [48, 1040], BF16)
    ptA = sb("ptA", [128, 6 * BLK], BF16)
    ptB = sb("ptB", [128, 6 * BLK], BF16)
    ptP = sb("ptP", [48, 512], BF16)
    den = sb("den", [128, 512], F32)
    x1slot = sb("x1slot", [128, 2048], F32)
    xhat = sb("xhatsb", [128, 2048], BF16)
    scrF = sb("scrF", [128, 512], F32)
    Hsb = sb("Hsb", [128, 512], BF16)
    Hsb2 = sb("Hsb2", [128, 512], BF16)
    mskT16 = sb("mskT16", [E, BLK], BF16)
    msksb = sb("msksb", [128, 2 * E], F32)
    maxsb = sb("maxsb", [128, 2], F32)
    wg16 = sb("wg16", [128, 8 * E], BF16)
    xb1 = xhat

    P0 = psmk("P0")
    P1 = psmk("P1")
    P2 = psmk("P2")
    P3 = psmk("P3")
    P4 = psmk("P4")
    P5 = psmk("P5")
    P6 = psmk("P6")
    P7 = psmk("P7")
    PY = [P2, P3, P4, P5]

    S = Sched()
    sems = {}
    for name in ("xd", "gd", "t", "v", "s", "g"):
        sems[name] = es.enter_context(nc.semaphore(name))
    S.sems = sems

    chunk_info = {}
    unit_mask_idx = {}
    for wb in range(NBLK_SEQ):
        gmin = max(0, 2 * wb - 4)
        chunk_info[wb] = [(gc, gc % RING, None)
                          for gc in range(gmin, 2 * wb + 2)]
        for ui in range((len(chunk_info[wb]) + 1) // 2):
            unit_mask_idx[(wb, ui)] = mask_map.get((wb, ui))

    pre_cast0 = None
    S.produce("gpsimd", lambda EE: EE.dma_start(wg16[:], dwg.rearrange("c p e -> p c e")), "gd", 16)
    S.produce("gpsimd", lambda EE: EE.dma_start(b2s[:], db2), "gd", 16)
    t_smalls = S.produce("gpsimd", lambda EE: EE.dma_start(e8f[:], de8), "gd", 16)
    t_part = {}
    for part in range(3):
        for dc in range(8):
            S.produce("gpsimd", lambda EE, dc=dc, part=part: EE.dma_start(
                wqkv[:, dc * 3072 + part * 1024: dc * 3072 + (part + 1) * 1024],
                dwqkv[dc][:, part * 1024:(part + 1) * 1024]), "gd", 16)
        t_part[part] = S.c("gd")
    for dc in range(8):
        S.produce("gpsimd", lambda EE, dc=dc: EE.dma_start(
            wout[:, dc * 1024:(dc + 1) * 1024], dwout[dc]), "gd", 16)
    t_wout = S.c("gd")
    for e in range(E):
        for hc in range(2):
            S.produce("gpsimd", lambda EE, e=e, hc=hc: EE.dma_start(
                w2p[:, (e * 2 + hc) * 1024:(e * 2 + hc + 1) * 1024], dw2[e, hc]), "gd", 16)
    t_w2 = S.c("gd")

    S.produce("sync", lambda EE: EE.dma_start(ident[:], dident), "xd", 16)
    S.produce("sync", lambda EE: EE.dma_start(b1tot[:], db1), "xd", 16)
    S.produce("sync", lambda EE: EE.dma_start(lnssb[:], dlns), "xd", 16)
    t_lnw = S.produce("sync", lambda EE: EE.dma_start(lnbsb[:], dlnb), "xd", 16)
    S.produce("sync", lambda EE: EE.dma_start(
        maskt[:], dmask.rearrange("n p t -> p n t")), "xd", 16)
    x_ready = {}
    x_ready[0] = S.produce("sync", lambda EE: EE.dma_start(xslot[:], dx[0]), "xd", 16)
    t_plain = S.c("xd")

    S.run("vector", lambda EE: EE.memset(cbias[:], -C_EXP))
    S.run("vector", lambda EE: EE.memset(onescol[:], 1.0 / D))
    S.run("vector", lambda EE: EE.memset(onescol16[:], 1.0 / D))
    S.run("vector", lambda EE: EE.memset(onesr16[:], 1.0))
    t_memset = S.produce("vector", lambda EE: EE.memset(
        bass.AP(vring, 64, [[RING * 1040, 128], [1040, RING], [65, NHEAD], [1, 1]]), 1.0),
        "v", 1)
    S.wait("vector", "xd", x_ready[0])
    pre_cast0_v = S.produce("vector", lambda EE: EE.tensor_copy(
        bfA[:], xslot[:]), "v", 1)

    S.wait("vector", "xd", t_lnw)
    S.wait("vector", "xd", t_plain)
    cons_v = {-2: 0, -1: 0}
    cons_t = {-2: 0, -1: 0}
    b1add = {-2: 0, -1: 0}
    for idx in range(16):
        e, h = idx // 2, idx % 2
        buf = idx % 2
        S.wait("sync", "v", max(cons_v[idx - 2], b1add[idx - 2]))
        S.wait("sync", "t", cons_t[idx - 2])
        stg = S.produce("sync", lambda EE, e=e, h=h, buf=buf: EE.dma_start(
            x1slot[:, buf * 1024:(buf + 1) * 1024],
            dw1[e].rearrange("c p h -> p c h")[:, 4 * h:4 * h + 4, :]), "xd", 16)
        S.wait("vector", "xd", stg)
        for k in range(4):
            dc = 4 * h + k
            cons_v[idx] = S.produce(
                "vector", lambda EE, e=e, dc=dc, k=k, buf=buf: EE.tensor_scalar_mul(
                    w1p[:, (e * 8 + dc) * HF:(e * 8 + dc + 1) * HF],
                    x1slot[:, buf * 1024 + k * HF: buf * 1024 + (k + 1) * HF],
                    lnssb[:, e * 8 + dc:e * 8 + dc + 1]), "v", 1)
        S.wait("tensor", "xd", stg)
        S.wait("tensor", "v", b1add[idx - 2])
        for hc in range(2):
            col = hc * 64 + (idx % 2) * 32
            for j in range(4):
                cons_t[idx] = S.produce(
                    "tensor",
                    lambda EE, e=e, h=h, hc=hc, j=j, buf=buf, col=col: EE.matmul(
                        P0[:, col:col + 1],
                        x1slot[:, buf * 1024 + j * HF + hc * 128:
                               buf * 1024 + j * HF + hc * 128 + 128],
                        lnbsb[:, e * 8 + 4 * h + j: e * 8 + 4 * h + j + 1],
                        start=(j == 0), stop=(j == 3)), "t", 1)
        S.wait("vector", "t", cons_t[idx])
        for hc in range(2):
            b1add[idx] = S.produce(
                "vector", lambda EE, e=e, col=hc * 64 + (idx % 2) * 32, hc=hc: EE.tensor_add(
                    b1tot[:, e * 2 + hc:e * 2 + hc + 1],
                    b1tot[:, e * 2 + hc:e * 2 + hc + 1],
                    P0[:, col:col + 1]), "v", 1)
    stage_v_final = S.c("v")

    if dbg:
        S.wait("gpsimd", "v", S.c("v"))
        S.produce("gpsimd", lambda EE: EE.dma_start(dbg_b1, b1tot[:]), "gd", 16)

    S.wait("tensor", "xd", t_plain)
    S.wait("tensor", "gd", t_smalls)
    S.wait("tensor", "v", t_memset)
    S.wait("scalar", "v", t_memset)

    qkv_done = {}
    proj_last_t = {}
    gate_done = {}
    ffn_free = {}

    def emit_qkv(b):
        wb_ = b % NBLK_SEQ
        s0_ = (2 * wb_) % RING
        if b == 0:
            cast_g = 0
            S.wait("tensor", "v", pre_cast0_v)
        else:
            S.wait("gpsimd", "t", proj_last_t.get(b - 1, 0))
            cast_g = S.produce("gpsimd", lambda EE, b=b: EE.dma_start(bfA[:], dx[b]), "gd", 16)
            S.wait("tensor", "gd", cast_g)
        fr = ffn_free.get(b - 1)
        if fr:
            S.wait("tensor", "s", fr[0])
            S.wait("tensor", "v", fr[1])
        PB = [P0, P1]
        bank_user = [("v", stage_v_final) if b == 0 else None, None]
        def bwait(bk):
            if bank_user[bk]:
                S.wait("tensor", bank_user[bk][0], bank_user[bk][1])
        for ec in range(8):
            bk = ec % 2
            if b == 0 and ec == 0:
                S.wait("tensor", "gd", t_part[0])
            bwait(bk)
            for dc in range(8):
                S.produce("tensor", lambda EE, ec=ec, dc=dc, bk=bk: EE.matmul(
                    PB[bk][:, 0:256],
                    wqkv[:, dc * 3072 + ec * 128: dc * 3072 + ec * 128 + 128],
                    bfA[:, dc * 256:(dc + 1) * 256],
                    start=(dc == 0), stop=(dc == 7)), "t", 1)
            S.wait("scalar", "t", S.c("t"))
            bank_user[bk] = ("s", S.produce("scalar", lambda EE, ec=ec, bk=bk: EE.copy(
                qsb[:, ec * 256:(ec + 1) * 256], PB[bk][:, 0:256]), "s", 1))
        for hc in range(8):
            bk = hc % 2
            if b == 0 and hc == 0:
                S.wait("tensor", "gd", t_part[1])
            bwait(bk)
            for dc in range(8):
                S.produce("tensor", lambda EE, hc=hc, dc=dc, bk=bk: EE.matmul(
                    PB[bk][:, 0:256],
                    wqkv[:, dc * 3072 + 1024 + hc * 128: dc * 3072 + 1024 + hc * 128 + 128],
                    bfA[:, dc * 256:(dc + 1) * 256],
                    start=(dc == 0), stop=(dc == 7)), "t", 1)
            S.wait("scalar", "t", S.c("t"))
            bank_user[bk] = ("s", S.produce("scalar", lambda EE, hc=hc, bk=bk, s0_=s0_: EE.copy(
                kring[:, hc * 768 + s0_ * 128: hc * 768 + s0_ * 128 + 256],
                PB[bk][:, 0:256]), "s", 1))
        for j in range(4):
            tsub, vh = j // 2, j % 2
            bk = j % 2
            vslot = s0_ + tsub
            if b == 0 and j == 0:
                S.wait("tensor", "gd", t_part[2])
            bwait(bk)
            for dc in range(8):
                S.produce("tensor", lambda EE, tsub=tsub, vh=vh, dc=dc, bk=bk: EE.matmul(
                    PB[bk][:, 0:512],
                    bfA[:, dc * 256 + tsub * 128: dc * 256 + tsub * 128 + 128],
                    wqkv[:, dc * 3072 + 2048 + vh * 512: dc * 3072 + 2048 + vh * 512 + 512],
                    start=(dc == 0), stop=(dc == 7)), "t", 1)
            S.wait("vector", "t", S.c("t"))
            bank_user[bk] = ("v", S.produce("vector", lambda EE, vslot=vslot, vh=vh, bk=bk: EE.tensor_copy(
                bass.AP(vring, vslot * 1040 + vh * 520,
                        [[RING * 1040, 128], [65, 8], [1, 64]]),
                bass.AP(PB[bk], 0, [[512, 128], [64, 8], [1, 64]])), "v", 1))
        if wb_ == 0:
            S.wait("vector", "s", S.c("s"))
            S.produce("vector", lambda EE: EE.tensor_copy(
                bass.AP(kpers, 0, [[8 * NP, 128], [NP, 8], [1, NP]]),
                bass.AP(kring, 0, [[8 * RING * 128, 128], [768, 8], [1, NP]])), "v", 1)
            vp_v = S.produce("vector", lambda EE: EE.tensor_copy(
                vpers[0:NP, :], vring[0:NP, 0:1040]), "v", 1)
            S.wait("sync", "v", vp_v)
            S.wait("sync", "t", persB_users[0])
            kpersB_x[0] = S.produce("sync", lambda EE: EE.dma_start(
                vpers[32:48, :], vring[0:NP, 0:1040]), "xd", 16)
        qkv_done[b] = (cast_g, S.c("s"), S.c("v"))

    kpersB_x = [0]
    persB_users = [0]
    persexp_done = {}
    emit_qkv(0)
    attn_prev_muls = [0]
    prev_ln3 = [0]

    out_dma = {}
    pair_free = [0, 0]

    for b in range(0 if startup_only else nblk_run):
        wb = b % NBLK_SEQ
        CH = chunk_info[wb]
        nch = len(CH)
        s0 = (2 * wb) % RING
        use_pers = (wb >= 3) and pers_mode >= 2
        use_pers_sc = (wb >= 3) and pers_mode >= 1

        sq_guard = {}
        sq_ready = {}

        def ln_sq(dc, src):
            sq_half = dc % 2
            g = sq_guard.get(sq_half, 0)
            if g:
                S.wait("vector", "t", g)
            sq_ready[dc] = S.produce("vector", lambda EE, dc=dc, sq_half=sq_half: EE.tensor_mul(
                sqb[:, sq_half * 256:sq_half * 256 + 256],
                src(dc), src(dc)), "v", 1)

        def ln_mm(dc, src, bank, mean16=None):
            sq_half = dc % 2
            S.wait("tensor", "v", sq_ready[dc])
            if mean16 is not None:
                S.produce("tensor", lambda EE, dc=dc, mean16=mean16: EE.matmul(
                    bank[0:1, 0:256], onescol16[:],
                    mean16[:, dc * 256:(dc + 1) * 256],
                    start=(dc == 0), stop=False), "t", 1)
            else:
                S.produce("tensor", lambda EE, dc=dc: EE.matmul(
                    bank[0:1, 0:256], onescol[:], src(dc),
                    start=(dc == 0), stop=False), "t", 1)
            sq_guard[sq_half] = S.produce("tensor", lambda EE, dc=dc, sq_half=sq_half: EE.matmul(
                bank[0:1, 256:512], onescol16[:],
                sqb[:, sq_half * 256:sq_half * 256 + 256],
                start=False, stop=(dc == 7)), "t", 1)

        def ln_finish(src, dst, bank, bcb, wide_src, wide_dst, pre_bc_wait=0, pre_bc_wait_v=0):
            S.wait("vector", "t", S.c("t"))
            S.produce("vector", lambda EE: EE.tensor_copy(den16[0:1, 0:256], bank[0:1, 0:256]), "v", 1)
            S.produce("vector", lambda EE: EE.tensor_mul(
                scrF[0:1, 256:512], den16[0:1, 0:256], den16[0:1, 0:256]), "v", 1)
            S.produce("vector", lambda EE: EE.tensor_sub(
                scrF[0:1, 256:512], bank[0:1, 256:512], scrF[0:1, 256:512]), "v", 1)
            var_v = S.produce("vector", lambda EE: EE.tensor_scalar_add(
                scrF[0:1, 256:512], scrF[0:1, 256:512], LN_EPS), "v", 1)
            S.wait("scalar", "v", var_v)
            S.produce("scalar", lambda EE: EE.activation(
                scrF[0:1, 256:512], scrF[0:1, 256:512], AF.Ln), "s", 1)
            r_s = S.produce("scalar", lambda EE: EE.activation(
                den16[0:1, 256:512], scrF[0:1, 256:512], AF.Exp, scale=-0.5), "s", 1)
            S.wait("tensor", "s", r_s)
            S.wait("tensor", "s", pre_bc_wait)
            S.wait("tensor", "v", pre_bc_wait_v)
            bc2 = S.produce("tensor", lambda EE: EE.matmul(
                bcb[:, 0:512], onesr16[0:1, 0:128], den16[0:1, 0:512],
                start=True, stop=True), "t", 1)
            S.wait("vector", "t", bc2)
            wsrc, wdst = wide_src, wide_dst
            halves = []
            for hf in range(2):
                S.produce("vector", lambda EE, hf=hf: EE.tensor_sub(
                    wdst[:, hf * 1024:(hf + 1) * 1024], wsrc[:, hf * 1024:(hf + 1) * 1024],
                    bass.AP(bcb, 0, [[512, 128], [0, 4], [1, 256]])), "v", 1)
                halves.append(S.produce("vector", lambda EE, hf=hf: EE.tensor_mul(
                    wdst[:, hf * 1024:(hf + 1) * 1024], wdst[:, hf * 1024:(hf + 1) * 1024],
                    bass.AP(bcb, 256, [[512, 128], [0, 4], [1, 256]])), "v", 1))
            ln_finish.halves = halves
            return S.c("v")


        if b not in qkv_done:
            emit_qkv(b)
        cast_g, qkv_s, qkv_vv = qkv_done[b]
        S.wait("tensor", "s", qkv_s)
        S.wait("tensor", "v", qkv_vv)

        if stage < 2:
            continue

        pairs = [(P2, P3), (P6, P7)]
        units = []
        for hp in range(8):
            ci = 0
            first = True
            for ui, i in enumerate(range(0, nch, 2)):
                grp = CH[i:i + 2]
                units.append((hp, grp, ci, first, i + len(grp) >= nch,
                              unit_mask_idx[(wb, ui)]))
                ci += len(grp)
                first = False
        NU = len(units)
        unit_tmm = {}
        unit_exp = {}
        unit_mask = {}
        unit_av = {}
        persexp_done = {}
        evac_done = {}
        muls_done = {-1: attn_prev_muls[0], -2: attn_prev_muls[0]}
        rcp_done = {}
        cp_done = {}
        bc_done = {}
        stagB_x = {}
        avflag = {}

        def sc_unit(k):
            hp, grp, ci, first, last, umi = units[k]
            pA_, pB_ = pairs[k % 2]
            if k >= 2:
                S.wait("tensor", "s", unit_exp.get(k - 2, 0))
            elif pair_free[k % 2]:
                S.wait("tensor", "v", pair_free[k % 2])
            for j, (gc, sl, mi) in enumerate(grp):
                S.produce("tensor", lambda EE, sl=sl, hp=hp, j=j, pA_=pA_: EE.matmul(
                    pA_[:, j * 256:j * 256 + 256],
                    kring[0:64, hp * 768 + sl * 128: hp * 768 + sl * 128 + 128],
                    qsb[0:64, hp * 256:hp * 256 + 256],
                    start=True, stop=True), "t", 1)
                S.produce("tensor", lambda EE, sl=sl, hp=hp, j=j, pB_=pB_: EE.matmul(
                    pB_[:, j * 256:j * 256 + 256],
                    kring[64:128, hp * 768 + sl * 128: hp * 768 + sl * 128 + 128],
                    qsb[64:128, hp * 256:hp * 256 + 256],
                    start=True, stop=True), "t", 1)
            unit_tmm[k] = S.c("t")

        last_persexp = [0]

        def pers_mm(hp):
            S.wait("tensor", "v", muls_done.get(hp - 1, 0))
            S.wait("tensor", "s", qkv_s)
            S.wait("tensor", "v", max(qkv_vv, prev_ln3[0]))
            S.wait("tensor", "xd", kpersB_x[0])
            for sub in range(2):
                h2 = hp + sub
                S.produce("tensor", lambda EE, h2=h2, sub=sub: EE.matmul(
                    P0[0:NP, sub * 256:sub * 256 + 256],
                    kpers[0:64, h2 * NP: h2 * NP + NP],
                    qsb[0:64, h2 * 256:h2 * 256 + 256],
                    start=True, stop=True), "t", 1)
            for sub in range(2):
                h2 = hp + sub
                pt_t = S.produce("tensor", lambda EE, h2=h2, sub=sub: EE.matmul(
                    P1[32:32 + NP, sub * 256:sub * 256 + 256],
                    kpers[64:128, h2 * NP: h2 * NP + NP],
                    qsb[64:128, h2 * 256:h2 * 256 + 256],
                    start=True, stop=True), "t", 1)
            S.wait("scalar", "t", pt_t)
            S.produce("scalar", lambda EE: EE.activation(
                ptP[0:NP, 0:512], P0[0:NP, 0:512],
                AF.Exp, scale=SCALE), "s", 1)
            pe_s = S.produce("scalar", lambda EE: EE.activation(
                ptP[32:32 + NP, 0:512], P1[32:32 + NP, 0:512],
                AF.Exp, scale=SCALE), "s", 1)
            persexp_done[hp] = pe_s
            persexp_done[hp + 1] = pe_s
            last_persexp[0] = pe_s
            persB_users[0] = pe_s

        def exp_unit(k):
            hp, grp, ci, first, last, umi = units[k]
            pA_, pB_ = pairs[k % 2]
            gl = len(grp)
            so = (k % 3) * 512
            S.wait("scalar", "t", unit_tmm[k])
            S.wait("scalar", "t", unit_av.get(k - 3, 0))
            S.produce("scalar", lambda EE, so=so, gl=gl, pA_=pA_: EE.activation(
                ptA[:, so: so + gl * 256], pA_[:, 0:gl * 256],
                AF.Exp, scale=SCALE), "s", 1)
            unit_exp[k] = S.produce("scalar", lambda EE, so=so, gl=gl, pB_=pB_: EE.activation(
                ptB[:, so: so + gl * 256], pB_[:, 0:gl * 256],
                AF.Exp, scale=SCALE), "s", 1)
            if umi is not None:
                S.wait("vector", "s", unit_exp[k])
                w = gl * 256
                for pt in (ptA, ptB):
                    S.produce("vector", lambda EE, pt=pt, so=so, w=w, umi=umi: EE.tensor_mul(
                        pt[:, so: so + w],
                        pt[:, so: so + w],
                        maskt[:, umi * 512: umi * 512 + w]), "v", 1)
            unit_mask[k] = S.c("v") if umi is not None else 0

        def av_unit(k):
            hp, grp, ci, first, last, umi = units[k]
            so = (k % 3) * 512
            S.wait("tensor", "s", unit_exp[k])
            if last and use_pers:
                S.wait("tensor", "s", persexp_done[hp])
            S.wait("tensor", "v", unit_mask[k])
            if first:
                S.wait("tensor", "v", evac_done.get(hp - 1, 0))
                avflag[hp] = True
            for j, (gc, sl, mi) in enumerate(grp):
                st = avflag[hp]
                avflag[hp] = False
                fin = last and (j == len(grp) - 1) and not use_pers
                S.produce("tensor", lambda EE, sl=sl, hp=hp, so=so, j=j, st=st, fin=fin: EE.matmul(
                    P4[0:65, 0:256],
                    vring[:, sl * 1040 + 2 * hp * 65: sl * 1040 + 2 * hp * 65 + 65],
                    ptA[:, so + j * 256: so + j * 256 + 256],
                    start=st, stop=fin), "t", 1)
                S.produce("tensor", lambda EE, sl=sl, hp=hp, so=so, j=j, st=st, fin=fin: EE.matmul(
                    P5[0:65, 0:256],
                    vring[:, sl * 1040 + (2 * hp + 1) * 65: sl * 1040 + (2 * hp + 1) * 65 + 65],
                    ptB[:, so + j * 256: so + j * 256 + 256],
                    start=st, stop=fin), "t", 1)
            if last and use_pers:
                po = (hp % 2) * 256
                S.produce("tensor", lambda EE, hp=hp, po=po: EE.matmul(
                    P4[0:65, 0:256], vpers[0:NP, 2 * hp * 65: 2 * hp * 65 + 65],
                    ptP[0:NP, po:po + 256], start=False, stop=True), "t", 1)
                S.produce("tensor", lambda EE, hp=hp, po=po: EE.matmul(
                    P5[0:65, 0:256],
                    vpers[32:32 + NP, (2 * hp + 1) * 65: (2 * hp + 1) * 65 + 65],
                    ptP[32:32 + NP, po:po + 256], start=False, stop=True), "t", 1)
            unit_av[k] = S.c("t")
            if last:
                db = den if hp % 2 == 0 else scrF
                S.wait("vector", "t", unit_av[k])
                S.wait("vector", "s", rcp_done.get(hp - 2, 0))
                S.produce("vector", lambda EE, db=db: EE.tensor_copy(
                    db[64:65, 0:256], P4[64:65, 0:256]), "v", 1)
                cp_v = S.produce("vector", lambda EE, db=db: EE.tensor_copy(
                    db[64:65, 256:512], P5[64:65, 0:256]), "v", 1)
                cp_done[hp] = cp_v
                S.produce("vector", lambda EE, hp=hp: EE.tensor_copy(
                    bfA[0:64, hp * 256:hp * 256 + 256], P4[0:64, 0:256]), "v", 1)
                evac_done[hp] = S.produce("vector", lambda EE, hp=hp: EE.tensor_copy(
                    xhat[0:64, hp * 256:hp * 256 + 256], P5[0:64, 0:256]), "v", 1)

        def tail_bc(hp):
            db = den if hp % 2 == 0 else scrF
            PB_ = P0 if hp < 6 else (P6 if hp == 6 else P7)
            S.wait("scalar", "v", cp_done[hp])
            S.wait("scalar", "t", bc_done.get(hp - 1, 0))
            S.produce("scalar", lambda EE, db=db: EE.activation(
                db[64:65, 0:512], db[64:65, 0:512], AF.Ln), "s", 1)
            rcp_done[hp] = S.produce("scalar", lambda EE, db=db: EE.activation(
                den16[64:65, 0:512], db[64:65, 0:512], AF.Exp, scale=-1.0), "s", 1)
            S.wait("tensor", "s", rcp_done[hp])
            if hp < 6:
                S.wait("tensor", "s", last_persexp[0])
                S.wait("tensor", "v", muls_done.get(hp - 1, 0))
            bc_t = S.produce("tensor", lambda EE, PB_=PB_: EE.matmul(
                PB_[:, 0:512], onesr16[64:65, 0:128], den16[64:65, 0:512],
                start=True, stop=True), "t", 1)
            bc_done[hp] = bc_t
            S.wait("vector", "t", bc_t)
            S.produce("vector", lambda EE, hp=hp, PB_=PB_: EE.tensor_mul(
                bfA[0:64, hp * 256:hp * 256 + 256],
                bfA[0:64, hp * 256:hp * 256 + 256], PB_[0:64, 0:256]), "v", 1)
            muls_done[hp] = S.produce("vector", lambda EE, hp=hp, PB_=PB_: EE.tensor_mul(
                xhat[0:64, hp * 256:hp * 256 + 256],
                xhat[0:64, hp * 256:hp * 256 + 256], PB_[0:64, 256:512]), "v", 1)
            S.wait("sync", "v", muls_done[hp])
            stagB_x[hp] = S.produce("sync", lambda EE, hp=hp: EE.dma_start(
                bfA[64:128, hp * 256:hp * 256 + 256],
                xhat[0:64, hp * 256:hp * 256 + 256]), "xd", 16)

        pend_q = []
        for k in range(NU):
            hp, grp, ci, first, last, umi = units[k]
            sc_unit(k)
            if k >= 1:
                av_unit(k - 1)
            if pend_q and k >= pend_q[0][1] + 3:
                tail_bc(pend_q.pop(0)[0])
            if first and use_pers_sc and hp % 2 == 0:
                pers_mm(hp)
            exp_unit(k)
            if k >= 1 and units[k - 1][4]:
                pend_q.append((units[k - 1][0], k - 1))
        av_unit(NU - 1)
        for hp_, _ in pend_q:
            tail_bc(hp_)
        tail_bc(7)
        attn_prev_muls[0] = muls_done[7]
        attn_v = S.c("v")
        if stage < 3:
            continue

        if dbg:
            S.wait("gpsimd", "xd", stagB_x[7])
            S.produce("gpsimd", lambda EE, b=b: EE.dma_start(dbg_attn[b], bfA[:]), "gd", 16)
            S.wait("gpsimd", "v", S.c("v"))
            S.produce("gpsimd", lambda EE, b=b: EE.dma_start(dbg_q[b], qsb[:]), "gd", 16)

        if b == 0:
            S.wait("tensor", "gd", t_wout)
        S.wait("tensor", "v", evac_done[7])
        srcz = lambda dc: xslot[:, dc * 256: dc * 256 + 256]
        S.wait("vector", "xd", x_ready.get(b, x_ready[0]))
        S.wait("tensor", "v", prev_ln3[0])
        PBp = [P0, P1]
        pbank_user = [("v", muls_done[5]), ("s", persexp_done.get(7, 0))]
        for ep in range(4):
            bk = ep % 2
            if pbank_user[bk] and pbank_user[bk][1]:
                S.wait("tensor", pbank_user[bk][0], pbank_user[bk][1])
            for half in range(2):
                ec = 2 * ep + half
                for dc in range(8):
                    if ep == 0 and half == 0:
                        S.wait("tensor", "xd", stagB_x[dc])
                    S.produce("tensor", lambda EE, ec=ec, dc=dc, bk=bk, half=half: EE.matmul(
                        PBp[bk][:, half * 256:half * 256 + 256],
                        wout[:, dc * 1024 + ec * 128: dc * 1024 + ec * 128 + 128],
                        bfA[:, dc * 256:(dc + 1) * 256],
                        start=(dc == 0), stop=(dc == 7)), "t", 1)
            S.wait("vector", "t", S.c("t"))
            pbank_user[bk] = ("v", S.produce("vector", lambda EE, ep=ep, bk=bk: EE.tensor_add(
                xslot[:, ep * 512: ep * 512 + 512],
                xslot[:, ep * 512: ep * 512 + 512],
                PBp[bk][:, 0:512]), "v", 1))
            for ec in (2 * ep, 2 * ep + 1):
                if ec >= 2:
                    ln_mm(ec - 2, srcz, P4)
                ln_sq(ec, srcz)
        ln_mm(6, srcz, P4)
        ln_mm(7, srcz, P4)
        proj_last_t[b] = S.c("t")

        if stage < 4:
            continue

        if b - 1 in out_dma:
            S.wait("vector", "gd", out_dma[b - 1])
        S.wait("vector", "xd", stagB_x[7])
        srcx1 = lambda dc: x1slot[:, dc * 256:(dc + 1) * 256]
        S.wait("vector", "t", S.c("t"))
        S.produce("vector", lambda EE: EE.tensor_copy(den16[0:1, 0:256], P4[0:1, 0:256]), "v", 1)
        S.produce("vector", lambda EE: EE.tensor_mul(
            scrF[0:1, 256:512], den16[0:1, 0:256], den16[0:1, 0:256]), "v", 1)
        S.produce("vector", lambda EE: EE.tensor_sub(
            scrF[0:1, 256:512], P4[0:1, 256:512], scrF[0:1, 256:512]), "v", 1)
        var_v = S.produce("vector", lambda EE: EE.tensor_scalar_add(
            scrF[0:1, 256:512], scrF[0:1, 256:512], LN_EPS), "v", 1)
        S.wait("scalar", "v", var_v)
        S.produce("scalar", lambda EE: EE.activation(
            scrF[0:1, 256:512], scrF[0:1, 256:512], AF.Ln), "s", 1)
        r_s1 = S.produce("scalar", lambda EE: EE.activation(
            den16[0:1, 256:512], scrF[0:1, 256:512], AF.Exp, scale=-0.5), "s", 1)
        S.wait("tensor", "s", r_s1)
        S.wait("tensor", "v", evac_done[7])
        bc1 = S.produce("tensor", lambda EE: EE.matmul(
            P2[:, 0:512], onesr16[0:1, 0:128], den16[0:1, 0:512],
            start=True, stop=True), "t", 1)
        S.wait("vector", "t", bc1)
        half_v = {}
        xb1h = {}
        for hf in range(2):
            S.produce("vector", lambda EE, hf=hf: EE.tensor_sub(
                x1slot[:, hf * 1024:(hf + 1) * 1024], xslot[:, hf * 1024:(hf + 1) * 1024],
                bass.AP(P2, 0, [[512, 128], [0, 4], [1, 256]])), "v", 1)
            half_v[hf] = S.produce("vector", lambda EE, hf=hf: EE.tensor_mul(
                x1slot[:, hf * 1024:(hf + 1) * 1024], x1slot[:, hf * 1024:(hf + 1) * 1024],
                bass.AP(P2, 256, [[512, 128], [0, 4], [1, 256]])), "v", 1)
            S.wait("scalar", "v", half_v[hf])
            xb1h[hf] = S.produce("scalar", lambda EE, hf=hf: EE.copy(
                xb1[:, hf * 1024:(hf + 1) * 1024],
                x1slot[:, hf * 1024:(hf + 1) * 1024]), "s", 1)
        wide1_v = half_v[1]
        xb1_v = xb1h[1]
        gate_started = [False]

        def gate_mms(dc):
            for tsub in range(2):
                S.produce("tensor", lambda EE, tsub=tsub, dc=dc, st=(dc == 0 and tsub == 0): EE.matmul(
                    P3[:, tsub * 8: tsub * 8 + 8],
                    xb1[:, dc * 256 + tsub * 128: dc * 256 + tsub * 128 + 128],
                    wg16[:, dc * 8:(dc + 1) * 8],
                    start=st, stop=(dc == 7 and tsub == 1)), "t", 1)

        for dc in range(8):
            if dc == 0:
                S.wait("tensor", "s", xb1h[0])
            if dc == 4:
                S.wait("tensor", "v", half_v[1])
            if dc == 6:
                S.wait("tensor", "s", xb1h[1])
            if dc >= 2:
                ln_mm(dc - 2, srcx1, P5, mean16=xb1)
                gate_mms(dc - 2)
            ln_sq(dc, srcx1)
        for dc in (6, 7):
            ln_mm(dc, srcx1, P5, mean16=xb1)
            gate_mms(dc)
        ln1_v = S.c("v")

        if b + 1 < NBLK:
            S.wait("sync", "v", ln1_v)
            x_ready[b + 1] = S.produce("sync", lambda EE, b=b: EE.dma_start(
                xslot[:], dx[b + 1]), "xd", 16)

        if dbg:
            S.wait("gpsimd", "v", ln1_v)
            S.produce("gpsimd", lambda EE, b=b: EE.dma_start(dbg_x1[b], x1slot[:]), "gd", 16)
        if stage < 5:
            continue

        ln2_v = ln_finish(srcx1,
                          lambda dc: xhat[:, dc * 256:(dc + 1) * 256], P5, P2,
                          x1slot, xhat)

        if dbg:
            S.wait("gpsimd", "v", ln2_v)
            S.produce("gpsimd", lambda EE, b=b: EE.dma_start(dbg_xh[b], xhat[:]), "gd", 16)

        gate_t = S.c("t")
        gate_done[b] = gate_t
        S.wait("vector", "t", gate_t)
        if dbg:
            lgdump_v = S.produce("vector", lambda EE: EE.tensor_copy(
                scrF[:, 0:16], P3[:, 0:16]), "v", 1)
            S.wait("gpsimd", "v", lgdump_v)
            S.produce("gpsimd", lambda EE, b=b: EE.dma_start(dbg_lg[b], scrF[:, 0:16]), "gd", 16)
        for tsub in range(2):
            rmx = S.produce("vector", lambda EE, tsub=tsub: EE.reduce_max(
                maxsb[:, tsub:tsub + 1], P3[:, tsub * 8:tsub * 8 + 8], axis=AX.X), "v", 1)
        S.wait("vector", "v", rmx)
        for tsub in range(2):
            S.produce("vector", lambda EE, tsub=tsub: EE.tensor_scalar(
                msksb[:, tsub * 8:tsub * 8 + 8], P3[:, tsub * 8:tsub * 8 + 8],
                maxsb[:, tsub:tsub + 1], None, ALU.is_ge), "v", 1)
        msk_v = S.c("v")
        if dbg:
            S.wait("gpsimd", "v", msk_v)
            S.produce("gpsimd", lambda EE, b=b: EE.dma_start(dbg_mx[b], maxsb[:]), "gd", 16)
        S.wait("tensor", "v", msk_v)
        for tsub in range(2):
            S.produce("tensor", lambda EE, tsub=tsub: EE.transpose(
                P5[0:8, tsub * 128:tsub * 128 + 128],
                msksb[:, tsub * 8:tsub * 8 + 8], ident[:]), "t", 1)
        mskT_t = S.c("t")
        S.wait("vector", "t", mskT_t)
        mskT_v = S.produce("vector", lambda EE: EE.tensor_copy(
            mskT16[0:8, 0:256], P5[0:8, 0:256]), "v", 1)
        if dbg:
            S.wait("gpsimd", "v", mskT_v)
            S.produce("gpsimd", lambda EE, b=b: EE.dma_start(dbg_msk[b], msksb[:]), "gd", 16)
            S.produce("gpsimd", lambda EE, b=b: EE.dma_start(dbg_mt[b], mskT16[0:8, 0:256]), "gd", 16)

        if stage < 7:
            continue

        HB = [P0, P7]
        gel_of = {}
        hm_of = {}
        first_y = [True] * 4

        def w1_mms(e):
            hb = HB[e % 2]
            S.wait("tensor", "s", gel_of.get(e - 2, 0))
            for hc in range(2):
                for dc in range(8):
                    if e == 0 and hc == 0 and dc in (0, 4):
                        S.wait("tensor", "v", ln_finish.halves[dc // 4])
                    S.produce("tensor", lambda EE, e=e, hc=hc, dc=dc, hb=hb: EE.matmul(
                        hb[:, hc * 256:hc * 256 + 256],
                        w1p[:, (e * 8 + dc) * HF + hc * 128:
                            (e * 8 + dc) * HF + hc * 128 + 128],
                        xhat[:, dc * 256:(dc + 1) * 256],
                        start=(dc == 0), stop=(dc == 7)), "t", 1)
            return S.c("t")

        def gelu_hm(e, h_t):
            hb = HB[e % 2]
            S.wait("tensor", "v", mskT_v if e == 0 else S.c("v"))
            bce_t = S.produce("tensor", lambda EE, e=e: EE.matmul(
                P1[:, 0:256], e8f[0:8, e * 128:(e + 1) * 128], mskT16[0:8, 0:256],
                start=True, stop=True), "t", 1)
            S.wait("scalar", "t", h_t)
            S.wait("scalar", "v", hm_of.get(e - 2, 0))
            hs = Hsb2 if e % 2 else Hsb
            for hc in range(2):
                S.produce("scalar", lambda EE, e=e, hc=hc, hb=hb, hs=hs: EE.activation(
                    hs[:, hc * 256:hc * 256 + 256], hb[:, hc * 256:hc * 256 + 256],
                    AF.Gelu, bias=b1tot[:, e * 2 + hc: e * 2 + hc + 1]), "s", 1)
            gel_of[e] = S.c("s")
            S.wait("vector", "s", gel_of[e])
            S.wait("vector", "t", bce_t)
            hm_of[e] = S.produce("vector", lambda EE, hs=hs: EE.tensor_mul(
                hs[:, 0:512], hs[:, 0:512],
                bass.AP(P1, 0, [[512, 128], [0, 2], [1, 256]])), "v", 1)

        def w2_mms(e):
            hs = Hsb2 if e % 2 else Hsb
            if b == 0 and e == 0:
                S.wait("tensor", "gd", t_w2)
            S.wait("tensor", "v", hm_of[e])
            for dc in range(8):
                yb = PY[dc // 2]
                yoff = (dc % 2) * 256
                for hc in range(2):
                    st = first_y[dc // 2]
                    first_y[dc // 2] = False
                    S.produce("tensor",
                              lambda EE, e=e, hc=hc, dc=dc, yb=yb, yoff=yoff, st=st, hs=hs: EE.matmul(
                                  yb[:, yoff:yoff + 256],
                                  w2p[:, (e * 2 + hc) * 1024 + dc * 128:
                                      (e * 2 + hc) * 1024 + dc * 128 + 128],
                                  hs[:, hc * 256:hc * 256 + 256],
                                  start=st, stop=False), "t", 1)

        ht = w1_mms(0)
        gelu_hm(0, ht)
        for e in range(1, E):
            ht = w1_mms(e)
            w2_mms(e - 1)
            gelu_hm(e, ht)
        w2_mms(E - 1)
        for dc in range(8):
            yb = PY[dc // 2]
            yoff = (dc % 2) * 256
            S.produce("tensor", lambda EE, dc=dc, yb=yb, yoff=yoff: EE.matmul(
                yb[:, yoff:yoff + 256],
                b2s[0:8, dc * 128:(dc + 1) * 128], mskT16[0:8, 0:256],
                start=False, stop=True), "t", 1)
        y_t = S.c("t")
        ffn_free[b] = (S.c("s"), S.c("v"))
        if b + 1 < nblk_run:
            emit_qkv(b + 1)
        S.wait("vector", "t", y_t)
        S.wait("tensor", "s", S.c("s"))
        for pp_ in range(4):
            S.produce("vector", lambda EE, pp_=pp_: EE.tensor_add(
                x1slot[:, pp_ * 512:(pp_ + 1) * 512],
                x1slot[:, pp_ * 512:(pp_ + 1) * 512], PY[pp_][:, 0:512]), "v", 1)
            for dc in (2 * pp_, 2 * pp_ + 1):
                if dc >= 2:
                    ln_mm(dc - 2, srcx1, P6)
                ln_sq(dc, srcx1)
        ln_mm(6, srcx1, P6)
        ln_mm(7, srcx1, P6)

        if dbg:
            S.wait("gpsimd", "v", S.c("v"))
            dmp2 = S.produce("gpsimd", lambda EE, b=b: EE.dma_start(dbg_z2[b], x1slot[:]), "gd", 16)
            S.wait("vector", "gd", dmp2)

        z2add_v = S.c("v")
        ln3_v = ln_finish(srcx1, srcx1, P6, P1, x1slot, x1slot,
                          pre_bc_wait=qkv_done[b + 1][1] if b + 1 in qkv_done else 0,
                          pre_bc_wait_v=qkv_done[b + 1][2] if b + 1 in qkv_done else 0)
        pair_free[0] = z2add_v
        pair_free[1] = ln3_v
        prev_ln3[0] = ln3_v

        S.wait("gpsimd", "v", ln3_v)
        out_dma[b] = S.produce("gpsimd", lambda EE, b=b: EE.dma_start(
            dy[b], x1slot[:]), "gd", 16)

    S.run("gpsimd", lambda EE: EE.wait_ge(sems["gd"], S.c("gd")))

    with nc.Block() as block:
        @block.sync
        def _(EE):
            for fn in S.ops["sync"]:
                fn(EE)

        @block.gpsimd
        def _(EE):
            for fn in S.ops["gpsimd"]:
                fn(EE)

        @block.vector
        def _(EE):
            for fn in S.ops["vector"]:
                fn(EE)

        @block.scalar
        def _(EE):
            for fn in S.ops["scalar"]:
                fn(EE)

        @block.tensor
        def _(EE):
            for fn in S.ops["tensor"]:
                fn(EE)

    build_program._last_S = S
    return nc, es


_CACHE = {}


def kernel(L0_attn_x, L0_attn_W_qkv, L0_attn_W_out, L0_ffn_Wg, L0_ffn_ln_scale,
           L0_ffn_ln_bias, L0_ffn_W1, L0_ffn_b1, L0_ffn_W2, L0_ffn_b2):
    x = np.ascontiguousarray(np.asarray(L0_attn_x, dtype=np.float32))
    Wqkv = np.ascontiguousarray(np.asarray(L0_attn_W_qkv, dtype=np.float32))
    Wout = np.ascontiguousarray(np.asarray(L0_attn_W_out, dtype=np.float32))
    Wg = np.ascontiguousarray(np.asarray(L0_ffn_Wg, dtype=np.float32))
    lns = np.ascontiguousarray(np.asarray(L0_ffn_ln_scale, dtype=np.float32))
    lnb = np.ascontiguousarray(np.asarray(L0_ffn_ln_bias, dtype=np.float32))
    W1 = np.ascontiguousarray(np.asarray(L0_ffn_W1, dtype=np.float32))
    b1 = np.ascontiguousarray(np.asarray(L0_ffn_b1, dtype=np.float32))
    W2 = np.ascontiguousarray(np.asarray(L0_ffn_W2, dtype=np.float32))
    b2 = np.ascontiguousarray(np.asarray(L0_ffn_b2, dtype=np.float32))

    mask_tiles, mask_map = _masks_for_blocks()

    if "prog" not in _CACHE:
        _CACHE["prog"] = build_program(mask_tiles, mask_map)
    nc, _es = _CACHE["prog"]

    wqkvT = np.ascontiguousarray(Wqkv.T.reshape(8, 128, 3072))
    woutT = np.ascontiguousarray(Wout.T.reshape(8, 128, 1024))
    w1h = np.ascontiguousarray(W1.reshape(E, 8, 128, HF))
    w2h = np.ascontiguousarray(W2.reshape(E, 2, 128, 1024))
    wgT = np.ascontiguousarray(Wg.T.reshape(8, 128, E))
    lnsh = np.ascontiguousarray(lns.reshape(E, 8, 128).transpose(2, 0, 1))
    lnbh = np.ascontiguousarray(lnb.reshape(E, 8, 128).transpose(2, 0, 1))
    b1h = np.ascontiguousarray(b1.reshape(E, 2, 128).transpose(2, 0, 1))
    maskh = np.ascontiguousarray(mask_tiles.astype(BF))
    identh = np.eye(128, dtype=np.float32)
    e8h = np.zeros((8, E, 128), dtype=np.float32)
    for e in range(E):
        e8h[e, e, :] = 1.0

    in_maps = []
    for c in range(NCORES):
        xc = x[SEQ_PER_CORE * c: SEQ_PER_CORE * (c + 1)].reshape(SEQ_PER_CORE * T, D)
        xfm = np.ascontiguousarray(xc.T)
        x5 = np.ascontiguousarray(
            xfm.reshape(8, 128, NBLK, BLK).transpose(2, 1, 0, 3))
        in_maps.append({
            "x5": x5, "wqkvT": wqkvT, "woutT": woutT, "w1h": w1h, "w2h": w2h,
            "wgT": wgT, "lnsh": lnsh, "lnbh": lnbh, "b1h": b1h, "b2h": b2,
            "maskh": maskh, "identh": identh, "e8h": e8h,
        })

    res = run_bass_kernel_spmd(nc, in_maps, core_ids=list(range(NCORES)))

    out = np.empty((NCORES * SEQ_PER_CORE, T, D), dtype=np.float32)
    for c in range(NCORES):
        y5 = np.asarray(res.results[c]["y5"])
        yfm = y5.transpose(2, 1, 0, 3).reshape(D, SEQ_PER_CORE * T)
        out[SEQ_PER_CORE * c: SEQ_PER_CORE * (c + 1)] = \
            yfm.T.reshape(SEQ_PER_CORE, T, D)
    return out


# version 43 (final)
# speedup vs baseline: 1.0379x; 1.0049x over previous
import numpy as np
import ml_dtypes
from contextlib import ExitStack

import concourse.bass as bass
import concourse.mybir as mybir
from concourse.bass_utils import run_bass_kernel_spmd

F32 = mybir.dt.float32
BF16 = mybir.dt.bfloat16
AF = mybir.ActivationFunctionType
ALU = mybir.AluOpType
AX = mybir.AxisListType
BF = ml_dtypes.bfloat16

D = 1024
T = 2048
NCORES = 8
SEQ_PER_CORE = 2
BLK = 256
NBLK_SEQ = T // BLK
NBLK = SEQ_PER_CORE * NBLK_SEQ
NHEAD = 16
DH = 64
WINDOW = 512
NP = 16
C_EXP = 16.0
SCALE = 0.125
LN_EPS = 1e-5
E = 8
HF = 256
RING = 6

ENGS = ("sync", "gpsimd", "vector", "scalar", "tensor")


def _masks_for_blocks():
    tiles, keys, mapping = [], {}, {}
    for wb in range(NBLK_SEQ):
        gmin = max(0, 2 * wb - 4)
        gcs = list(range(gmin, 2 * wb + 2))
        for ui, i in enumerate(range(0, len(gcs), 2)):
            pair = gcs[i:i + 2]
            cols = []
            for gc in pair:
                k = 128 * gc + np.arange(128)[:, None]
                q = 256 * wb + np.arange(256)[None, :]
                band = (np.abs(q - k) <= WINDOW) & (q >= NP) & (k >= NP) & (q >= k)
                cols.append(((k < NP) | band))
            allowed = np.concatenate(cols, axis=1)
            if allowed.all():
                mapping[(wb, ui)] = None
                continue
            key = allowed.tobytes()
            if key not in keys:
                keys[key] = len(tiles)
                tiles.append(allowed.astype(np.float32))
            mapping[(wb, ui)] = keys[key]
    return np.stack(tiles), mapping


class Sched:
    def __init__(self):
        self.ops = {e: [] for e in ENGS}
        self.cnt = {}
        self.sems = {}
        self.log = {e: [] for e in ENGS}

    def produce(self, eng, fn, sem, inc=1):
        self.cnt[sem] = self.cnt.get(sem, 0) + inc
        tgt = self.cnt[sem]

        def wrapped(EE, fn=fn, sem=sem, inc=inc):
            fn(EE).then_inc(self.sems[sem], inc)
        self.ops[eng].append(wrapped)
        self.log[eng].append(("i", sem, inc))
        return tgt

    def run(self, eng, fn):
        self.ops[eng].append(fn)

    def wait(self, eng, sem, val):
        if val is None or val <= 0:
            return
        have = self.cnt.get(sem, 0)
        assert val <= have, f"wait {eng} {sem} {val} > emitted {have}"
        self.ops[eng].append(lambda EE, sem=sem, val=val: EE.wait_ge(self.sems[sem], val))
        self.log[eng].append(("w", sem, val))

    def c(self, sem):
        return self.cnt.get(sem, 0)

    def simulate(self):
        pos = {e: 0 for e in ENGS}
        val = {}
        progress = True
        while progress:
            progress = False
            for e in ENGS:
                while pos[e] < len(self.log[e]):
                    kind, sem, x = self.log[e][pos[e]]
                    if kind == "i":
                        val[sem] = val.get(sem, 0) + x
                    else:
                        if val.get(sem, 0) < x:
                            break
                    pos[e] += 1
                    progress = True
        stuck = {e: (pos[e], len(self.log[e]), self.log[e][pos[e]] if pos[e] < len(self.log[e]) else None)
                 for e in ENGS if pos[e] < len(self.log[e])}
        return stuck or None


def build_program(mask_tiles, mask_map, nblk_run=NBLK, startup_only=False, stage=99, qkvs=9, dbg=False, pers_mode=2):
    NM = mask_tiles.shape[0]
    nc = bass.Bass("TRN2", target_bir_lowering=False)

    dx = nc.dram_tensor("x5", [NBLK, 128, 8, BLK], F32, kind="ExternalInput").ap()
    dy = nc.dram_tensor("y5", [NBLK, 128, 8, BLK], F32, kind="ExternalOutput").ap()
    dwqkv = nc.dram_tensor("wqkvT", [8, 128, 3072], F32, kind="ExternalInput").ap()
    dwout = nc.dram_tensor("woutT", [8, 128, 1024], F32, kind="ExternalInput").ap()
    dw1 = nc.dram_tensor("w1h", [E, 8, 128, HF], F32, kind="ExternalInput").ap()
    dw2 = nc.dram_tensor("w2h", [E, 2, 128, 1024], F32, kind="ExternalInput").ap()
    dwg = nc.dram_tensor("wgT", [8, 128, E], F32, kind="ExternalInput").ap()
    dlns = nc.dram_tensor("lnsh", [128, E, 8], F32, kind="ExternalInput").ap()
    dlnb = nc.dram_tensor("lnbh", [128, E, 8], F32, kind="ExternalInput").ap()
    db1 = nc.dram_tensor("b1h", [128, E, 2], F32, kind="ExternalInput").ap()
    db2 = nc.dram_tensor("b2h", [E, 1024], F32, kind="ExternalInput").ap()
    dmask = nc.dram_tensor("maskh", [NM, 128, 2 * BLK], BF16, kind="ExternalInput").ap()
    dident = nc.dram_tensor("identh", [128, 128], F32, kind="ExternalInput").ap()
    if dbg:
        dbg_attn = nc.dram_tensor("dbg_attn", [NBLK, 128, 2048], BF16, kind="ExternalOutput").ap()
        dbg_x1 = nc.dram_tensor("dbg_x1", [NBLK, 128, 2048], F32, kind="ExternalOutput").ap()
        dbg_xh = nc.dram_tensor("dbg_xh", [NBLK, 128, 2048], BF16, kind="ExternalOutput").ap()
        dbg_q = nc.dram_tensor("dbg_q", [NBLK, 128, 2048], BF16, kind="ExternalOutput").ap()
        dbg_msk = nc.dram_tensor("dbg_msk", [NBLK, 128, 16], F32, kind="ExternalOutput").ap()
        dbg_mt = nc.dram_tensor("dbg_mt", [NBLK, 8, 256], BF16, kind="ExternalOutput").ap()
        dbg_h = nc.dram_tensor("dbg_h", [NBLK, 128, 512], BF16, kind="ExternalOutput").ap()
        dbg_z2 = nc.dram_tensor("dbg_z2", [NBLK, 128, 2048], F32, kind="ExternalOutput").ap()
        dbg_lg = nc.dram_tensor("dbg_lg", [NBLK, 128, 16], F32, kind="ExternalOutput").ap()
        dbg_mx = nc.dram_tensor("dbg_mx", [NBLK, 128, 2], F32, kind="ExternalOutput").ap()
        dbg_b1 = nc.dram_tensor("dbg_b1", [128, 16], F32, kind="ExternalOutput").ap()
    de8 = nc.dram_tensor("e8h", [8, E, 128], F32, kind="ExternalInput").ap()

    es = ExitStack()

    def sb(name, shape, dt=F32):
        return es.enter_context(nc.sbuf_tensor(name, shape, dt))

    def psmk(name):
        return es.enter_context(nc.psum_tensor(name, [128, 512], F32))

    wqkv = sb("wqkv", [128, 8 * 3072], BF16)
    wout = sb("wout", [128, 8 * 1024], BF16)
    w1p = sb("w1p", [128, E * 8 * HF], BF16)
    w2p = sb("w2p", [128, E * 2 * 1024], BF16)
    lnssb = sb("lnssb", [128, E * 8], F32)
    lnbsb = sb("lnbsb", [128, E * 8], F32)
    b1tot = sb("b1tot", [128, E * 2], F32)
    b2s = sb("b2s", [E, 1024], BF16)
    e8f = sb("e8f", [8, E * 128], BF16)
    maskt = sb("maskt", [128, NM * 2 * BLK], BF16)
    ident = sb("identsb", [128, 128], F32)
    cbias = sb("cbias", [128, 1], F32)
    onescol = sb("onescol", [128, 1], F32)
    onescol16 = sb("onescol16", [128, 1], BF16)
    sqb = sb("sqb", [128, 512], BF16)
    onesr16 = sb("onesr16", [128, 128], BF16)
    den16 = sb("den16", [128, 512], BF16)

    xslot = sb("xslot", [128, 2048], F32)
    bfA = sb("bfA", [128, 2048], BF16)
    qsb = sb("qsb", [128, 8 * BLK], BF16)
    kring = sb("kringsb", [128, 8 * RING * 128], BF16)
    kpers = sb("kpers", [128, 8 * NP], BF16)
    vring = sb("vringsb", [128, RING * 1040], BF16)
    vpers = sb("vpers", [48, 1040], BF16)
    ptAB = sb("ptAB", [128, 12 * BLK], BF16)
    ptP = sb("ptP", [48, 512], BF16)
    den = sb("den", [128, 512], F32)
    x1slot = sb("x1slot", [128, 2048], F32)
    xhat = sb("xhatsb", [128, 2048], BF16)
    scrF = sb("scrF", [128, 512], F32)
    Hsb = sb("Hsb", [128, 512], BF16)
    Hsb2 = sb("Hsb2", [128, 512], BF16)
    mskT16 = sb("mskT16", [E, BLK], BF16)
    msksb = sb("msksb", [128, 2 * E], F32)
    maxsb = sb("maxsb", [128, 2], F32)
    wg16 = sb("wg16", [128, 8 * E], BF16)
    xb1 = xhat

    P0 = psmk("P0")
    P1 = psmk("P1")
    P23 = es.enter_context(nc.psum_tensor("P23", [128, 1024], F32))
    P4 = psmk("P4")
    P5 = psmk("P5")
    P67 = es.enter_context(nc.psum_tensor("P67", [128, 1024], F32))

    class PV:
        def __init__(self, t, co):
            self.t, self.co, self.w = t, co, 1024
        def __getitem__(self, idx):
            rs, cs = idx
            return self.t[rs, slice(cs.start + self.co, cs.stop + self.co)]

    P2 = PV(P23, 0)
    P3 = PV(P23, 512)
    P6 = PV(P67, 0)
    P7 = PV(P67, 512)
    PY = [P2, P3, P4, P5]

    S = Sched()
    sems = {}
    for name in ("xd", "gd", "t", "v", "s", "g"):
        sems[name] = es.enter_context(nc.semaphore(name))
    S.sems = sems

    chunk_info = {}
    unit_mask_idx = {}
    for wb in range(NBLK_SEQ):
        gmin = max(0, 2 * wb - 4)
        chunk_info[wb] = [(gc, gc % RING, None)
                          for gc in range(gmin, 2 * wb + 2)]
        for ui in range((len(chunk_info[wb]) + 1) // 2):
            unit_mask_idx[(wb, ui)] = mask_map.get((wb, ui))

    pre_cast0 = None
    S.produce("gpsimd", lambda EE: EE.dma_start(wg16[:], dwg.rearrange("c p e -> p c e")), "gd", 16)
    S.produce("gpsimd", lambda EE: EE.dma_start(b2s[:], db2), "gd", 16)
    t_smalls = S.produce("gpsimd", lambda EE: EE.dma_start(e8f[:], de8), "gd", 16)
    t_part = {}
    for part in range(3):
        for dc in range(8):
            S.produce("gpsimd", lambda EE, dc=dc, part=part: EE.dma_start(
                wqkv[:, dc * 3072 + part * 1024: dc * 3072 + (part + 1) * 1024],
                dwqkv[dc][:, part * 1024:(part + 1) * 1024]), "gd", 16)
        t_part[part] = S.c("gd")
    for dc in range(8):
        S.produce("gpsimd", lambda EE, dc=dc: EE.dma_start(
            wout[:, dc * 1024:(dc + 1) * 1024], dwout[dc]), "gd", 16)
    t_wout = S.c("gd")
    for e in range(E):
        for hc in range(2):
            S.produce("gpsimd", lambda EE, e=e, hc=hc: EE.dma_start(
                w2p[:, (e * 2 + hc) * 1024:(e * 2 + hc + 1) * 1024], dw2[e, hc]), "gd", 16)
    t_w2 = S.c("gd")

    S.produce("sync", lambda EE: EE.dma_start(ident[:], dident), "xd", 16)
    S.produce("sync", lambda EE: EE.dma_start(b1tot[:], db1), "xd", 16)
    S.produce("sync", lambda EE: EE.dma_start(lnssb[:], dlns), "xd", 16)
    t_lnw = S.produce("sync", lambda EE: EE.dma_start(lnbsb[:], dlnb), "xd", 16)
    S.produce("sync", lambda EE: EE.dma_start(
        maskt[:], dmask.rearrange("n p t -> p n t")), "xd", 16)
    x_ready = {}
    x_ready[0] = S.produce("sync", lambda EE: EE.dma_start(xslot[:], dx[0]), "xd", 16)
    t_plain = S.c("xd")

    S.run("vector", lambda EE: EE.memset(cbias[:], -C_EXP))
    S.run("vector", lambda EE: EE.memset(onescol[:], 1.0 / D))
    S.run("vector", lambda EE: EE.memset(onescol16[:], 1.0 / D))
    S.run("vector", lambda EE: EE.memset(onesr16[:], 1.0))
    t_memset = S.produce("vector", lambda EE: EE.memset(
        bass.AP(vring, 64, [[RING * 1040, 128], [1040, RING], [65, NHEAD], [1, 1]]), 1.0),
        "v", 1)
    S.wait("vector", "xd", x_ready[0])
    pre_cast0_v = S.produce("vector", lambda EE: EE.tensor_copy(
        bfA[:], xslot[:]), "v", 1)

    S.wait("vector", "xd", t_lnw)
    S.wait("vector", "xd", t_plain)
    cons_v = {-2: 0, -1: 0}
    cons_t = {-2: 0, -1: 0}
    b1add = {-2: 0, -1: 0}
    for idx in range(16):
        e, h = idx // 2, idx % 2
        buf = idx % 2
        S.wait("sync", "v", max(cons_v[idx - 2], b1add[idx - 2]))
        S.wait("sync", "t", cons_t[idx - 2])
        stg = S.produce("sync", lambda EE, e=e, h=h, buf=buf: EE.dma_start(
            x1slot[:, buf * 1024:(buf + 1) * 1024],
            dw1[e].rearrange("c p h -> p c h")[:, 4 * h:4 * h + 4, :]), "xd", 16)
        S.wait("vector", "xd", stg)
        for k in range(4):
            dc = 4 * h + k
            cons_v[idx] = S.produce(
                "vector", lambda EE, e=e, dc=dc, k=k, buf=buf: EE.tensor_scalar_mul(
                    w1p[:, (e * 8 + dc) * HF:(e * 8 + dc + 1) * HF],
                    x1slot[:, buf * 1024 + k * HF: buf * 1024 + (k + 1) * HF],
                    lnssb[:, e * 8 + dc:e * 8 + dc + 1]), "v", 1)
        S.wait("tensor", "xd", stg)
        S.wait("tensor", "v", b1add[idx - 2])
        for hc in range(2):
            col = hc * 64 + (idx % 2) * 32
            for j in range(4):
                cons_t[idx] = S.produce(
                    "tensor",
                    lambda EE, e=e, h=h, hc=hc, j=j, buf=buf, col=col: EE.matmul(
                        P0[:, col:col + 1],
                        x1slot[:, buf * 1024 + j * HF + hc * 128:
                               buf * 1024 + j * HF + hc * 128 + 128],
                        lnbsb[:, e * 8 + 4 * h + j: e * 8 + 4 * h + j + 1],
                        start=(j == 0), stop=(j == 3)), "t", 1)
        S.wait("vector", "t", cons_t[idx])
        for hc in range(2):
            b1add[idx] = S.produce(
                "vector", lambda EE, e=e, col=hc * 64 + (idx % 2) * 32, hc=hc: EE.tensor_add(
                    b1tot[:, e * 2 + hc:e * 2 + hc + 1],
                    b1tot[:, e * 2 + hc:e * 2 + hc + 1],
                    P0[:, col:col + 1]), "v", 1)
    stage_v_final = S.c("v")

    if dbg:
        S.wait("gpsimd", "v", S.c("v"))
        S.produce("gpsimd", lambda EE: EE.dma_start(dbg_b1, b1tot[:]), "gd", 16)

    S.wait("tensor", "xd", t_plain)
    S.wait("tensor", "gd", t_smalls)
    S.wait("tensor", "v", t_memset)
    S.wait("scalar", "v", t_memset)

    qkv_done = {}
    proj_last_t = {}
    gate_done = {}
    ffn_free = {}

    def emit_qkv(b):
        wb_ = b % NBLK_SEQ
        s0_ = (2 * wb_) % RING
        if b == 0:
            cast_g = 0
            S.wait("tensor", "v", pre_cast0_v)
        else:
            S.wait("gpsimd", "t", proj_last_t.get(b - 1, 0))
            cast_g = S.produce("gpsimd", lambda EE, b=b: EE.dma_start(bfA[:], dx[b]), "gd", 16)
            S.wait("tensor", "gd", cast_g)
        fr = ffn_free.get(b - 1)
        if fr:
            S.wait("tensor", "s", fr[0])
            S.wait("tensor", "v", fr[1])
        PB = [P0, P1]
        bank_user = [("v", stage_v_final) if b == 0 else None, None]
        def bwait(bk):
            if bank_user[bk]:
                S.wait("tensor", bank_user[bk][0], bank_user[bk][1])
        for ec in range(8):
            bk = ec % 2
            if b == 0 and ec == 0:
                S.wait("tensor", "gd", t_part[0])
            bwait(bk)
            for dc in range(8):
                S.produce("tensor", lambda EE, ec=ec, dc=dc, bk=bk: EE.matmul(
                    PB[bk][:, 0:256],
                    wqkv[:, dc * 3072 + ec * 128: dc * 3072 + ec * 128 + 128],
                    bfA[:, dc * 256:(dc + 1) * 256],
                    start=(dc == 0), stop=(dc == 7)), "t", 1)
            S.wait("scalar", "t", S.c("t"))
            bank_user[bk] = ("s", S.produce("scalar", lambda EE, ec=ec, bk=bk: EE.copy(
                qsb[:, ec * 256:(ec + 1) * 256], PB[bk][:, 0:256]), "s", 1))
        for hc in range(8):
            bk = hc % 2
            if b == 0 and hc == 0:
                S.wait("tensor", "gd", t_part[1])
            bwait(bk)
            for dc in range(8):
                S.produce("tensor", lambda EE, hc=hc, dc=dc, bk=bk: EE.matmul(
                    PB[bk][:, 0:256],
                    wqkv[:, dc * 3072 + 1024 + hc * 128: dc * 3072 + 1024 + hc * 128 + 128],
                    bfA[:, dc * 256:(dc + 1) * 256],
                    start=(dc == 0), stop=(dc == 7)), "t", 1)
            S.wait("scalar", "t", S.c("t"))
            bank_user[bk] = ("s", S.produce("scalar", lambda EE, hc=hc, bk=bk, s0_=s0_: EE.copy(
                kring[:, hc * 768 + s0_ * 128: hc * 768 + s0_ * 128 + 256],
                PB[bk][:, 0:256]), "s", 1))
        for j in range(4):
            tsub, vh = j // 2, j % 2
            bk = j % 2
            vslot = s0_ + tsub
            if b == 0 and j == 0:
                S.wait("tensor", "gd", t_part[2])
            bwait(bk)
            for dc in range(8):
                S.produce("tensor", lambda EE, tsub=tsub, vh=vh, dc=dc, bk=bk: EE.matmul(
                    PB[bk][:, 0:512],
                    bfA[:, dc * 256 + tsub * 128: dc * 256 + tsub * 128 + 128],
                    wqkv[:, dc * 3072 + 2048 + vh * 512: dc * 3072 + 2048 + vh * 512 + 512],
                    start=(dc == 0), stop=(dc == 7)), "t", 1)
            S.wait("vector", "t", S.c("t"))
            bank_user[bk] = ("v", S.produce("vector", lambda EE, vslot=vslot, vh=vh, bk=bk: EE.tensor_copy(
                bass.AP(vring, vslot * 1040 + vh * 520,
                        [[RING * 1040, 128], [65, 8], [1, 64]]),
                bass.AP(PB[bk], 0, [[512, 128], [64, 8], [1, 64]])), "v", 1))
        if wb_ == 0:
            S.wait("vector", "s", S.c("s"))
            S.produce("vector", lambda EE: EE.tensor_copy(
                bass.AP(kpers, 0, [[8 * NP, 128], [NP, 8], [1, NP]]),
                bass.AP(kring, 0, [[8 * RING * 128, 128], [768, 8], [1, NP]])), "v", 1)
            vp_v = S.produce("vector", lambda EE: EE.tensor_copy(
                vpers[0:NP, :], vring[0:NP, 0:1040]), "v", 1)
            S.wait("sync", "v", vp_v)
            S.wait("sync", "t", persB_users[0])
            kpersB_x[0] = S.produce("sync", lambda EE: EE.dma_start(
                vpers[32:48, :], vring[0:NP, 0:1040]), "xd", 16)
        qkv_done[b] = (cast_g, S.c("s"), S.c("v"))

    kpersB_x = [0]
    persB_users = [0]
    persexp_done = {}
    emit_qkv(0)
    attn_prev_muls = [0]
    prev_ln3 = [0]

    out_dma = {}
    pair_free = [0, 0]

    for b in range(0 if startup_only else nblk_run):
        wb = b % NBLK_SEQ
        CH = chunk_info[wb]
        nch = len(CH)
        s0 = (2 * wb) % RING
        use_pers = (wb >= 3) and pers_mode >= 2
        use_pers_sc = (wb >= 3) and pers_mode >= 1

        sq_guard = {}
        sq_ready = {}

        def ln_sq(dc, src):
            sq_half = dc % 2
            g = sq_guard.get(sq_half, 0)
            if g:
                S.wait("vector", "t", g)
            sq_ready[dc] = S.produce("vector", lambda EE, dc=dc, sq_half=sq_half: EE.tensor_mul(
                sqb[:, sq_half * 256:sq_half * 256 + 256],
                src(dc), src(dc)), "v", 1)

        def ln_mm(dc, src, bank, mean16=None):
            sq_half = dc % 2
            S.wait("tensor", "v", sq_ready[dc])
            if mean16 is not None:
                S.produce("tensor", lambda EE, dc=dc, mean16=mean16: EE.matmul(
                    bank[0:1, 0:256], onescol16[:],
                    mean16[:, dc * 256:(dc + 1) * 256],
                    start=(dc == 0), stop=False), "t", 1)
            else:
                S.produce("tensor", lambda EE, dc=dc: EE.matmul(
                    bank[0:1, 0:256], onescol[:], src(dc),
                    start=(dc == 0), stop=False), "t", 1)
            sq_guard[sq_half] = S.produce("tensor", lambda EE, dc=dc, sq_half=sq_half: EE.matmul(
                bank[0:1, 256:512], onescol16[:],
                sqb[:, sq_half * 256:sq_half * 256 + 256],
                start=False, stop=(dc == 7)), "t", 1)

        def ln_finish(src, dst, bank, bcb, wide_src, wide_dst, pre_bc_wait=0, pre_bc_wait_v=0):
            S.wait("vector", "t", S.c("t"))
            S.produce("vector", lambda EE: EE.tensor_copy(den16[0:1, 0:256], bank[0:1, 0:256]), "v", 1)
            S.produce("vector", lambda EE: EE.tensor_mul(
                scrF[0:1, 256:512], den16[0:1, 0:256], den16[0:1, 0:256]), "v", 1)
            S.produce("vector", lambda EE: EE.tensor_sub(
                scrF[0:1, 256:512], bank[0:1, 256:512], scrF[0:1, 256:512]), "v", 1)
            var_v = S.produce("vector", lambda EE: EE.tensor_scalar_add(
                scrF[0:1, 256:512], scrF[0:1, 256:512], LN_EPS), "v", 1)
            S.wait("scalar", "v", var_v)
            S.produce("scalar", lambda EE: EE.activation(
                scrF[0:1, 256:512], scrF[0:1, 256:512], AF.Ln), "s", 1)
            r_s = S.produce("scalar", lambda EE: EE.activation(
                den16[0:1, 256:512], scrF[0:1, 256:512], AF.Exp, scale=-0.5), "s", 1)
            S.wait("tensor", "s", r_s)
            S.wait("tensor", "s", pre_bc_wait)
            S.wait("tensor", "v", pre_bc_wait_v)
            bc2 = S.produce("tensor", lambda EE: EE.matmul(
                bcb[:, 0:512], onesr16[0:1, 0:128], den16[0:1, 0:512],
                start=True, stop=True), "t", 1)
            S.wait("vector", "t", bc2)
            wsrc, wdst = wide_src, wide_dst
            bt = bcb.t if isinstance(bcb, PV) else bcb
            bco = bcb.co if isinstance(bcb, PV) else 0
            bw = bcb.w if isinstance(bcb, PV) else 512
            halves = []
            for hf in range(2):
                S.produce("vector", lambda EE, hf=hf: EE.tensor_sub(
                    wdst[:, hf * 1024:(hf + 1) * 1024], wsrc[:, hf * 1024:(hf + 1) * 1024],
                    bass.AP(bt, bco, [[bw, 128], [0, 4], [1, 256]])), "v", 1)
                halves.append(S.produce("vector", lambda EE, hf=hf: EE.tensor_mul(
                    wdst[:, hf * 1024:(hf + 1) * 1024], wdst[:, hf * 1024:(hf + 1) * 1024],
                    bass.AP(bt, bco + 256, [[bw, 128], [0, 4], [1, 256]])), "v", 1))
            ln_finish.halves = halves
            return S.c("v")


        if b not in qkv_done:
            emit_qkv(b)
        cast_g, qkv_s, qkv_vv = qkv_done[b]
        S.wait("tensor", "s", qkv_s)
        S.wait("tensor", "v", qkv_vv)

        if stage < 2:
            continue

        pairs = [P23, P67]
        units = []
        for hp in range(8):
            ci = 0
            first = True
            for ui, i in enumerate(range(0, nch, 2)):
                grp = CH[i:i + 2]
                units.append((hp, grp, ci, first, i + len(grp) >= nch,
                              unit_mask_idx[(wb, ui)]))
                ci += len(grp)
                first = False
        NU = len(units)
        unit_tmm = {}
        unit_exp = {}
        unit_mask = {}
        unit_av = {}
        persexp_done = {}
        evac_done = {}
        muls_done = {-1: attn_prev_muls[0], -2: attn_prev_muls[0]}
        rcp_done = {}
        cp_done = {}
        bc_done = {}
        stagB_x = {}
        avflag = {}

        def sc_unit(k):
            hp, grp, ci, first, last, umi = units[k]
            PT = pairs[k % 2]
            if k >= 2:
                S.wait("tensor", "s", unit_exp.get(k - 2, 0))
            elif pair_free[k % 2]:
                S.wait("tensor", "v", pair_free[k % 2])
            for j, (gc, sl, mi) in enumerate(grp):
                S.produce("tensor", lambda EE, sl=sl, hp=hp, j=j, PT=PT: EE.matmul(
                    PT[:, j * 256:j * 256 + 256],
                    kring[0:64, hp * 768 + sl * 128: hp * 768 + sl * 128 + 128],
                    qsb[0:64, hp * 256:hp * 256 + 256],
                    start=True, stop=True), "t", 1)
                S.produce("tensor", lambda EE, sl=sl, hp=hp, j=j, PT=PT: EE.matmul(
                    PT[:, 512 + j * 256:512 + j * 256 + 256],
                    kring[64:128, hp * 768 + sl * 128: hp * 768 + sl * 128 + 128],
                    qsb[64:128, hp * 256:hp * 256 + 256],
                    start=True, stop=True), "t", 1)
            unit_tmm[k] = S.c("t")

        last_persexp = [0]

        def pers_mm(hp):
            S.wait("tensor", "v", muls_done.get(hp - 1, 0))
            S.wait("tensor", "s", qkv_s)
            S.wait("tensor", "v", max(qkv_vv, prev_ln3[0]))
            S.wait("tensor", "xd", kpersB_x[0])
            for sub in range(2):
                h2 = hp + sub
                S.produce("tensor", lambda EE, h2=h2, sub=sub: EE.matmul(
                    P0[0:NP, sub * 256:sub * 256 + 256],
                    kpers[0:64, h2 * NP: h2 * NP + NP],
                    qsb[0:64, h2 * 256:h2 * 256 + 256],
                    start=True, stop=True), "t", 1)
            for sub in range(2):
                h2 = hp + sub
                pt_t = S.produce("tensor", lambda EE, h2=h2, sub=sub: EE.matmul(
                    P1[32:32 + NP, sub * 256:sub * 256 + 256],
                    kpers[64:128, h2 * NP: h2 * NP + NP],
                    qsb[64:128, h2 * 256:h2 * 256 + 256],
                    start=True, stop=True), "t", 1)
            S.wait("scalar", "t", pt_t)
            S.produce("scalar", lambda EE: EE.activation(
                ptP[0:NP, 0:512], P0[0:NP, 0:512],
                AF.Exp, scale=SCALE), "s", 1)
            pe_s = S.produce("scalar", lambda EE: EE.activation(
                ptP[32:32 + NP, 0:512], P1[32:32 + NP, 0:512],
                AF.Exp, scale=SCALE), "s", 1)
            persexp_done[hp] = pe_s
            persexp_done[hp + 1] = pe_s
            last_persexp[0] = pe_s
            persB_users[0] = pe_s

        def exp_unit(k):
            hp, grp, ci, first, last, umi = units[k]
            PT = pairs[k % 2]
            so = (k % 3) * 1024
            S.wait("scalar", "t", unit_tmm[k])
            S.wait("scalar", "t", unit_av.get(k - 3, 0))
            unit_exp[k] = S.produce("scalar", lambda EE, so=so, PT=PT: EE.activation(
                ptAB[:, so: so + 1024], PT[:, 0:1024],
                AF.Exp, scale=SCALE), "s", 1)
            if umi is not None:
                S.wait("vector", "s", unit_exp[k])
                S.produce("vector", lambda EE, so=so, umi=umi: EE.tensor_mul(
                    ptAB[:, so: so + 1024],
                    ptAB[:, so: so + 1024],
                    bass.AP(maskt, umi * 512, [[NM * 512, 128], [0, 2], [1, 512]])), "v", 1)
            unit_mask[k] = S.c("v") if umi is not None else 0

        def av_unit(k):
            hp, grp, ci, first, last, umi = units[k]
            so = (k % 3) * 1024
            S.wait("tensor", "s", unit_exp[k])
            if last and use_pers:
                S.wait("tensor", "s", persexp_done[hp])
            S.wait("tensor", "v", unit_mask[k])
            if first:
                S.wait("tensor", "v", evac_done.get(hp - 1, 0))
                avflag[hp] = True
            for j, (gc, sl, mi) in enumerate(grp):
                st = avflag[hp]
                avflag[hp] = False
                fin = last and (j == len(grp) - 1) and not use_pers
                S.produce("tensor", lambda EE, sl=sl, hp=hp, so=so, j=j, st=st, fin=fin: EE.matmul(
                    P4[0:65, 0:256],
                    vring[:, sl * 1040 + 2 * hp * 65: sl * 1040 + 2 * hp * 65 + 65],
                    ptAB[:, so + j * 256: so + j * 256 + 256],
                    start=st, stop=fin), "t", 1)
                S.produce("tensor", lambda EE, sl=sl, hp=hp, so=so, j=j, st=st, fin=fin: EE.matmul(
                    P5[0:65, 0:256],
                    vring[:, sl * 1040 + (2 * hp + 1) * 65: sl * 1040 + (2 * hp + 1) * 65 + 65],
                    ptAB[:, so + 512 + j * 256: so + 512 + j * 256 + 256],
                    start=st, stop=fin), "t", 1)
            if last and use_pers:
                po = (hp % 2) * 256
                S.produce("tensor", lambda EE, hp=hp, po=po: EE.matmul(
                    P4[0:65, 0:256], vpers[0:NP, 2 * hp * 65: 2 * hp * 65 + 65],
                    ptP[0:NP, po:po + 256], start=False, stop=True), "t", 1)
                S.produce("tensor", lambda EE, hp=hp, po=po: EE.matmul(
                    P5[0:65, 0:256],
                    vpers[32:32 + NP, (2 * hp + 1) * 65: (2 * hp + 1) * 65 + 65],
                    ptP[32:32 + NP, po:po + 256], start=False, stop=True), "t", 1)
            unit_av[k] = S.c("t")
            if last:
                db = den if hp % 2 == 0 else scrF
                S.wait("vector", "t", unit_av[k])
                S.wait("vector", "s", rcp_done.get(hp - 2, 0))
                S.produce("vector", lambda EE, db=db: EE.tensor_copy(
                    db[64:65, 0:256], P4[64:65, 0:256]), "v", 1)
                cp_v = S.produce("vector", lambda EE, db=db: EE.tensor_copy(
                    db[64:65, 256:512], P5[64:65, 0:256]), "v", 1)
                cp_done[hp] = cp_v
                S.produce("vector", lambda EE, hp=hp: EE.tensor_copy(
                    bfA[0:64, hp * 256:hp * 256 + 256], P4[0:64, 0:256]), "v", 1)
                evac_done[hp] = S.produce("vector", lambda EE, hp=hp: EE.tensor_copy(
                    xhat[0:64, hp * 256:hp * 256 + 256], P5[0:64, 0:256]), "v", 1)

        def tail_bc(hp):
            db = den if hp % 2 == 0 else scrF
            PB_ = P0 if hp < 6 else (P6 if hp == 6 else P7)
            S.wait("scalar", "v", cp_done[hp])
            S.wait("scalar", "t", bc_done.get(hp - 1, 0))
            S.produce("scalar", lambda EE, db=db: EE.activation(
                db[64:65, 0:512], db[64:65, 0:512], AF.Ln), "s", 1)
            rcp_done[hp] = S.produce("scalar", lambda EE, db=db: EE.activation(
                den16[64:65, 0:512], db[64:65, 0:512], AF.Exp, scale=-1.0), "s", 1)
            S.wait("tensor", "s", rcp_done[hp])
            if hp < 6:
                S.wait("tensor", "s", last_persexp[0])
                S.wait("tensor", "v", muls_done.get(hp - 1, 0))
            bc_t = S.produce("tensor", lambda EE, PB_=PB_: EE.matmul(
                PB_[:, 0:512], onesr16[64:65, 0:128], den16[64:65, 0:512],
                start=True, stop=True), "t", 1)
            bc_done[hp] = bc_t
            S.wait("vector", "t", bc_t)
            S.produce("vector", lambda EE, hp=hp, PB_=PB_: EE.tensor_mul(
                bfA[0:64, hp * 256:hp * 256 + 256],
                bfA[0:64, hp * 256:hp * 256 + 256], PB_[0:64, 0:256]), "v", 1)
            muls_done[hp] = S.produce("vector", lambda EE, hp=hp, PB_=PB_: EE.tensor_mul(
                xhat[0:64, hp * 256:hp * 256 + 256],
                xhat[0:64, hp * 256:hp * 256 + 256], PB_[0:64, 256:512]), "v", 1)
            S.wait("sync", "v", muls_done[hp])
            stagB_x[hp] = S.produce("sync", lambda EE, hp=hp: EE.dma_start(
                bfA[64:128, hp * 256:hp * 256 + 256],
                xhat[0:64, hp * 256:hp * 256 + 256]), "xd", 16)

        pend_q = []
        for k in range(NU):
            hp, grp, ci, first, last, umi = units[k]
            sc_unit(k)
            if k >= 1:
                av_unit(k - 1)
            if pend_q and k >= pend_q[0][1] + 3:
                tail_bc(pend_q.pop(0)[0])
            if first and use_pers_sc and hp % 2 == 0:
                pers_mm(hp)
            exp_unit(k)
            if k >= 1 and units[k - 1][4]:
                pend_q.append((units[k - 1][0], k - 1))
        av_unit(NU - 1)
        for hp_, _ in pend_q:
            tail_bc(hp_)
        tail_bc(7)
        attn_prev_muls[0] = muls_done[7]
        attn_v = S.c("v")
        if stage < 3:
            continue

        if dbg:
            S.wait("gpsimd", "xd", stagB_x[7])
            S.produce("gpsimd", lambda EE, b=b: EE.dma_start(dbg_attn[b], bfA[:]), "gd", 16)
            S.wait("gpsimd", "v", S.c("v"))
            S.produce("gpsimd", lambda EE, b=b: EE.dma_start(dbg_q[b], qsb[:]), "gd", 16)

        if b == 0:
            S.wait("tensor", "gd", t_wout)
        S.wait("tensor", "v", evac_done[7])
        srcz = lambda dc: xslot[:, dc * 256: dc * 256 + 256]
        S.wait("vector", "xd", x_ready.get(b, x_ready[0]))
        S.wait("tensor", "v", prev_ln3[0])
        PBp = [P0, P1]
        pbank_user = [("v", muls_done[5]), ("s", persexp_done.get(7, 0))]
        for ep in range(4):
            bk = ep % 2
            if pbank_user[bk] and pbank_user[bk][1]:
                S.wait("tensor", pbank_user[bk][0], pbank_user[bk][1])
            for half in range(2):
                ec = 2 * ep + half
                for dc in range(8):
                    if ep == 0 and half == 0:
                        S.wait("tensor", "xd", stagB_x[dc])
                    S.produce("tensor", lambda EE, ec=ec, dc=dc, bk=bk, half=half: EE.matmul(
                        PBp[bk][:, half * 256:half * 256 + 256],
                        wout[:, dc * 1024 + ec * 128: dc * 1024 + ec * 128 + 128],
                        bfA[:, dc * 256:(dc + 1) * 256],
                        start=(dc == 0), stop=(dc == 7)), "t", 1)
            S.wait("vector", "t", S.c("t"))
            pbank_user[bk] = ("v", S.produce("vector", lambda EE, ep=ep, bk=bk: EE.tensor_add(
                xslot[:, ep * 512: ep * 512 + 512],
                xslot[:, ep * 512: ep * 512 + 512],
                PBp[bk][:, 0:512]), "v", 1))
            for ec in (2 * ep, 2 * ep + 1):
                if ec >= 2:
                    ln_mm(ec - 2, srcz, P4)
                ln_sq(ec, srcz)
        ln_mm(6, srcz, P4)
        ln_mm(7, srcz, P4)
        proj_last_t[b] = S.c("t")

        if stage < 4:
            continue

        if b - 1 in out_dma:
            S.wait("vector", "gd", out_dma[b - 1])
        S.wait("vector", "xd", stagB_x[7])
        srcx1 = lambda dc: x1slot[:, dc * 256:(dc + 1) * 256]
        S.wait("vector", "t", S.c("t"))
        S.produce("vector", lambda EE: EE.tensor_copy(den16[0:1, 0:256], P4[0:1, 0:256]), "v", 1)
        S.produce("vector", lambda EE: EE.tensor_mul(
            scrF[0:1, 256:512], den16[0:1, 0:256], den16[0:1, 0:256]), "v", 1)
        S.produce("vector", lambda EE: EE.tensor_sub(
            scrF[0:1, 256:512], P4[0:1, 256:512], scrF[0:1, 256:512]), "v", 1)
        var_v = S.produce("vector", lambda EE: EE.tensor_scalar_add(
            scrF[0:1, 256:512], scrF[0:1, 256:512], LN_EPS), "v", 1)
        S.wait("scalar", "v", var_v)
        S.produce("scalar", lambda EE: EE.activation(
            scrF[0:1, 256:512], scrF[0:1, 256:512], AF.Ln), "s", 1)
        r_s1 = S.produce("scalar", lambda EE: EE.activation(
            den16[0:1, 256:512], scrF[0:1, 256:512], AF.Exp, scale=-0.5), "s", 1)
        S.wait("tensor", "s", r_s1)
        S.wait("tensor", "v", evac_done[7])
        bc1 = S.produce("tensor", lambda EE: EE.matmul(
            P2[:, 0:512], onesr16[0:1, 0:128], den16[0:1, 0:512],
            start=True, stop=True), "t", 1)
        S.wait("vector", "t", bc1)
        half_v = {}
        xb1h = {}
        for hf in range(2):
            S.produce("vector", lambda EE, hf=hf: EE.tensor_sub(
                x1slot[:, hf * 1024:(hf + 1) * 1024], xslot[:, hf * 1024:(hf + 1) * 1024],
                bass.AP(P23, 0, [[1024, 128], [0, 4], [1, 256]])), "v", 1)
            half_v[hf] = S.produce("vector", lambda EE, hf=hf: EE.tensor_mul(
                x1slot[:, hf * 1024:(hf + 1) * 1024], x1slot[:, hf * 1024:(hf + 1) * 1024],
                bass.AP(P23, 256, [[1024, 128], [0, 4], [1, 256]])), "v", 1)
            S.wait("scalar", "v", half_v[hf])
            xb1h[hf] = S.produce("scalar", lambda EE, hf=hf: EE.copy(
                xb1[:, hf * 1024:(hf + 1) * 1024],
                x1slot[:, hf * 1024:(hf + 1) * 1024]), "s", 1)
        wide1_v = half_v[1]
        xb1_v = xb1h[1]
        gate_started = [False]

        def gate_mms(dc):
            for tsub in range(2):
                S.produce("tensor", lambda EE, tsub=tsub, dc=dc, st=(dc == 0 and tsub == 0): EE.matmul(
                    P3[:, tsub * 8: tsub * 8 + 8],
                    xb1[:, dc * 256 + tsub * 128: dc * 256 + tsub * 128 + 128],
                    wg16[:, dc * 8:(dc + 1) * 8],
                    start=st, stop=(dc == 7 and tsub == 1)), "t", 1)

        for dc in range(8):
            if dc == 0:
                S.wait("tensor", "s", xb1h[0])
            if dc == 4:
                S.wait("tensor", "v", half_v[1])
            if dc == 6:
                S.wait("tensor", "s", xb1h[1])
            if dc >= 2:
                ln_mm(dc - 2, srcx1, P5, mean16=xb1)
                gate_mms(dc - 2)
            ln_sq(dc, srcx1)
        for dc in (6, 7):
            ln_mm(dc, srcx1, P5, mean16=xb1)
            gate_mms(dc)
        ln1_v = S.c("v")

        if b + 1 < NBLK:
            S.wait("sync", "v", ln1_v)
            x_ready[b + 1] = S.produce("sync", lambda EE, b=b: EE.dma_start(
                xslot[:], dx[b + 1]), "xd", 16)

        if dbg:
            S.wait("gpsimd", "v", ln1_v)
            S.produce("gpsimd", lambda EE, b=b: EE.dma_start(dbg_x1[b], x1slot[:]), "gd", 16)
        if stage < 5:
            continue

        ln2_v = ln_finish(srcx1,
                          lambda dc: xhat[:, dc * 256:(dc + 1) * 256], P5, P2,
                          x1slot, xhat)

        if dbg:
            S.wait("gpsimd", "v", ln2_v)
            S.produce("gpsimd", lambda EE, b=b: EE.dma_start(dbg_xh[b], xhat[:]), "gd", 16)

        gate_t = S.c("t")
        gate_done[b] = gate_t
        S.wait("vector", "t", gate_t)
        if dbg:
            lgdump_v = S.produce("vector", lambda EE: EE.tensor_copy(
                scrF[:, 0:16], P3[:, 0:16]), "v", 1)
            S.wait("gpsimd", "v", lgdump_v)
            S.produce("gpsimd", lambda EE, b=b: EE.dma_start(dbg_lg[b], scrF[:, 0:16]), "gd", 16)
        for tsub in range(2):
            rmx = S.produce("vector", lambda EE, tsub=tsub: EE.reduce_max(
                maxsb[:, tsub:tsub + 1], P3[:, tsub * 8:tsub * 8 + 8], axis=AX.X), "v", 1)
        S.wait("vector", "v", rmx)
        for tsub in range(2):
            S.produce("vector", lambda EE, tsub=tsub: EE.tensor_scalar(
                msksb[:, tsub * 8:tsub * 8 + 8], P3[:, tsub * 8:tsub * 8 + 8],
                maxsb[:, tsub:tsub + 1], None, ALU.is_ge), "v", 1)
        msk_v = S.c("v")
        if dbg:
            S.wait("gpsimd", "v", msk_v)
            S.produce("gpsimd", lambda EE, b=b: EE.dma_start(dbg_mx[b], maxsb[:]), "gd", 16)
        S.wait("tensor", "v", msk_v)
        for tsub in range(2):
            S.produce("tensor", lambda EE, tsub=tsub: EE.transpose(
                P5[0:8, tsub * 128:tsub * 128 + 128],
                msksb[:, tsub * 8:tsub * 8 + 8], ident[:]), "t", 1)
        mskT_t = S.c("t")
        S.wait("vector", "t", mskT_t)
        mskT_v = S.produce("vector", lambda EE: EE.tensor_copy(
            mskT16[0:8, 0:256], P5[0:8, 0:256]), "v", 1)
        if dbg:
            S.wait("gpsimd", "v", mskT_v)
            S.produce("gpsimd", lambda EE, b=b: EE.dma_start(dbg_msk[b], msksb[:]), "gd", 16)
            S.produce("gpsimd", lambda EE, b=b: EE.dma_start(dbg_mt[b], mskT16[0:8, 0:256]), "gd", 16)

        if stage < 7:
            continue

        HB = [P0, P7]
        gel_of = {}
        hm_of = {}
        first_y = [True] * 4

        def w1_mms(e):
            hb = HB[e % 2]
            S.wait("tensor", "s", gel_of.get(e - 2, 0))
            for hc in range(2):
                for dc in range(8):
                    if e == 0 and hc == 0 and dc in (0, 4):
                        S.wait("tensor", "v", ln_finish.halves[dc // 4])
                    S.produce("tensor", lambda EE, e=e, hc=hc, dc=dc, hb=hb: EE.matmul(
                        hb[:, hc * 256:hc * 256 + 256],
                        w1p[:, (e * 8 + dc) * HF + hc * 128:
                            (e * 8 + dc) * HF + hc * 128 + 128],
                        xhat[:, dc * 256:(dc + 1) * 256],
                        start=(dc == 0), stop=(dc == 7)), "t", 1)
            return S.c("t")

        def gelu_hm(e, h_t):
            hb = HB[e % 2]
            S.wait("tensor", "v", mskT_v if e == 0 else S.c("v"))
            bce_t = S.produce("tensor", lambda EE, e=e: EE.matmul(
                P1[:, 0:256], e8f[0:8, e * 128:(e + 1) * 128], mskT16[0:8, 0:256],
                start=True, stop=True), "t", 1)
            S.wait("scalar", "t", h_t)
            S.wait("scalar", "v", hm_of.get(e - 2, 0))
            hs = Hsb2 if e % 2 else Hsb
            for hc in range(2):
                S.produce("scalar", lambda EE, e=e, hc=hc, hb=hb, hs=hs: EE.activation(
                    hs[:, hc * 256:hc * 256 + 256], hb[:, hc * 256:hc * 256 + 256],
                    AF.Gelu, bias=b1tot[:, e * 2 + hc: e * 2 + hc + 1]), "s", 1)
            gel_of[e] = S.c("s")
            S.wait("vector", "s", gel_of[e])
            S.wait("vector", "t", bce_t)
            hm_of[e] = S.produce("vector", lambda EE, hs=hs: EE.tensor_mul(
                hs[:, 0:512], hs[:, 0:512],
                bass.AP(P1, 0, [[512, 128], [0, 2], [1, 256]])), "v", 1)

        def w2_mms(e):
            hs = Hsb2 if e % 2 else Hsb
            if b == 0 and e == 0:
                S.wait("tensor", "gd", t_w2)
            S.wait("tensor", "v", hm_of[e])
            for dc in range(8):
                yb = PY[dc // 2]
                yoff = (dc % 2) * 256
                for hc in range(2):
                    st = first_y[dc // 2]
                    first_y[dc // 2] = False
                    S.produce("tensor",
                              lambda EE, e=e, hc=hc, dc=dc, yb=yb, yoff=yoff, st=st, hs=hs: EE.matmul(
                                  yb[:, yoff:yoff + 256],
                                  w2p[:, (e * 2 + hc) * 1024 + dc * 128:
                                      (e * 2 + hc) * 1024 + dc * 128 + 128],
                                  hs[:, hc * 256:hc * 256 + 256],
                                  start=st, stop=False), "t", 1)

        ht = w1_mms(0)
        gelu_hm(0, ht)
        for e in range(1, E):
            ht = w1_mms(e)
            w2_mms(e - 1)
            gelu_hm(e, ht)
        w2_mms(E - 1)
        for dc in range(8):
            yb = PY[dc // 2]
            yoff = (dc % 2) * 256
            S.produce("tensor", lambda EE, dc=dc, yb=yb, yoff=yoff: EE.matmul(
                yb[:, yoff:yoff + 256],
                b2s[0:8, dc * 128:(dc + 1) * 128], mskT16[0:8, 0:256],
                start=False, stop=True), "t", 1)
        y_t = S.c("t")
        ffn_free[b] = (S.c("s"), S.c("v"))
        if b + 1 < nblk_run:
            emit_qkv(b + 1)
        S.wait("vector", "t", y_t)
        S.wait("tensor", "s", S.c("s"))
        for pp_ in range(4):
            S.produce("vector", lambda EE, pp_=pp_: EE.tensor_add(
                x1slot[:, pp_ * 512:(pp_ + 1) * 512],
                x1slot[:, pp_ * 512:(pp_ + 1) * 512], PY[pp_][:, 0:512]), "v", 1)
            for dc in (2 * pp_, 2 * pp_ + 1):
                if dc >= 2:
                    ln_mm(dc - 2, srcx1, P6)
                ln_sq(dc, srcx1)
        ln_mm(6, srcx1, P6)
        ln_mm(7, srcx1, P6)

        if dbg:
            S.wait("gpsimd", "v", S.c("v"))
            dmp2 = S.produce("gpsimd", lambda EE, b=b: EE.dma_start(dbg_z2[b], x1slot[:]), "gd", 16)
            S.wait("vector", "gd", dmp2)

        z2add_v = S.c("v")
        ln3_v = ln_finish(srcx1, srcx1, P6, P1, x1slot, x1slot,
                          pre_bc_wait=qkv_done[b + 1][1] if b + 1 in qkv_done else 0,
                          pre_bc_wait_v=qkv_done[b + 1][2] if b + 1 in qkv_done else 0)
        pair_free[0] = z2add_v
        pair_free[1] = ln3_v
        prev_ln3[0] = ln3_v

        S.wait("gpsimd", "v", ln3_v)
        out_dma[b] = S.produce("gpsimd", lambda EE, b=b: EE.dma_start(
            dy[b], x1slot[:]), "gd", 16)

    S.run("gpsimd", lambda EE: EE.wait_ge(sems["gd"], S.c("gd")))

    with nc.Block() as block:
        @block.sync
        def _(EE):
            for fn in S.ops["sync"]:
                fn(EE)

        @block.gpsimd
        def _(EE):
            for fn in S.ops["gpsimd"]:
                fn(EE)

        @block.vector
        def _(EE):
            for fn in S.ops["vector"]:
                fn(EE)

        @block.scalar
        def _(EE):
            for fn in S.ops["scalar"]:
                fn(EE)

        @block.tensor
        def _(EE):
            for fn in S.ops["tensor"]:
                fn(EE)

    build_program._last_S = S
    return nc, es


_CACHE = {}


def kernel(L0_attn_x, L0_attn_W_qkv, L0_attn_W_out, L0_ffn_Wg, L0_ffn_ln_scale,
           L0_ffn_ln_bias, L0_ffn_W1, L0_ffn_b1, L0_ffn_W2, L0_ffn_b2):
    x = np.ascontiguousarray(np.asarray(L0_attn_x, dtype=np.float32))
    Wqkv = np.ascontiguousarray(np.asarray(L0_attn_W_qkv, dtype=np.float32))
    Wout = np.ascontiguousarray(np.asarray(L0_attn_W_out, dtype=np.float32))
    Wg = np.ascontiguousarray(np.asarray(L0_ffn_Wg, dtype=np.float32))
    lns = np.ascontiguousarray(np.asarray(L0_ffn_ln_scale, dtype=np.float32))
    lnb = np.ascontiguousarray(np.asarray(L0_ffn_ln_bias, dtype=np.float32))
    W1 = np.ascontiguousarray(np.asarray(L0_ffn_W1, dtype=np.float32))
    b1 = np.ascontiguousarray(np.asarray(L0_ffn_b1, dtype=np.float32))
    W2 = np.ascontiguousarray(np.asarray(L0_ffn_W2, dtype=np.float32))
    b2 = np.ascontiguousarray(np.asarray(L0_ffn_b2, dtype=np.float32))

    mask_tiles, mask_map = _masks_for_blocks()

    if "prog" not in _CACHE:
        _CACHE["prog"] = build_program(mask_tiles, mask_map)
    nc, _es = _CACHE["prog"]

    wqkvT = np.ascontiguousarray(Wqkv.T.reshape(8, 128, 3072))
    woutT = np.ascontiguousarray(Wout.T.reshape(8, 128, 1024))
    w1h = np.ascontiguousarray(W1.reshape(E, 8, 128, HF))
    w2h = np.ascontiguousarray(W2.reshape(E, 2, 128, 1024))
    wgT = np.ascontiguousarray(Wg.T.reshape(8, 128, E))
    lnsh = np.ascontiguousarray(lns.reshape(E, 8, 128).transpose(2, 0, 1))
    lnbh = np.ascontiguousarray(lnb.reshape(E, 8, 128).transpose(2, 0, 1))
    b1h = np.ascontiguousarray(b1.reshape(E, 2, 128).transpose(2, 0, 1))
    maskh = np.ascontiguousarray(mask_tiles.astype(BF))
    identh = np.eye(128, dtype=np.float32)
    e8h = np.zeros((8, E, 128), dtype=np.float32)
    for e in range(E):
        e8h[e, e, :] = 1.0

    in_maps = []
    for c in range(NCORES):
        xc = x[SEQ_PER_CORE * c: SEQ_PER_CORE * (c + 1)].reshape(SEQ_PER_CORE * T, D)
        xfm = np.ascontiguousarray(xc.T)
        x5 = np.ascontiguousarray(
            xfm.reshape(8, 128, NBLK, BLK).transpose(2, 1, 0, 3))
        in_maps.append({
            "x5": x5, "wqkvT": wqkvT, "woutT": woutT, "w1h": w1h, "w2h": w2h,
            "wgT": wgT, "lnsh": lnsh, "lnbh": lnbh, "b1h": b1h, "b2h": b2,
            "maskh": maskh, "identh": identh, "e8h": e8h,
        })

    res = run_bass_kernel_spmd(nc, in_maps, core_ids=list(range(NCORES)))

    out = np.empty((NCORES * SEQ_PER_CORE, T, D), dtype=np.float32)
    for c in range(NCORES):
        y5 = np.asarray(res.results[c]["y5"])
        yfm = y5.transpose(2, 1, 0, 3).reshape(D, SEQ_PER_CORE * T)
        out[SEQ_PER_CORE * c: SEQ_PER_CORE * (c + 1)] = \
            yfm.T.reshape(SEQ_PER_CORE, T, D)
    return out
